# version 31
# baseline (speedup 1.0000x reference)
import numpy as np
import ml_dtypes

import concourse.bass as bass
import concourse.mybir as mybir
from concourse.bass_utils import run_bass_kernel_spmd

F32 = mybir.dt.float32
BF16 = mybir.dt.bfloat16
AF = mybir.ActivationFunctionType
OP = mybir.AluOpType
BFNP = ml_dtypes.bfloat16

S = 2048
DM = 1024
NH = 16
DH = 64
NE = 8
NT = 16
NDC = 8
CSOFT = 15.0
SMSCALE = 0.125

ENGINES = ("tensor", "vector", "scalar", "sync", "gpsimd")


class Res:
    __slots__ = ("name", "writer", "readers", "overlays", "absorbed")

    def __init__(self, name, overlays=()):
        self.name = name
        self.writer = None
        self.readers = []
        self.overlays = list(overlays)
        self.absorbed = False

    def _absorb(self):
        if self.absorbed or not self.overlays:
            return []
        self.absorbed = True
        deps = []
        for o in self.overlays:
            if o.writer is not None:
                deps.append(o.writer)
            deps.extend(o.readers)
        return deps


class Task:
    __slots__ = ("engine", "emit", "deps", "need_inc", "seq", "dma", "qidx",
                 "rec_idx", "label", "sem_idx", "small")

    def __init__(self, engine, emit, dma, label):
        self.engine = engine
        self.emit = emit
        self.deps = []
        self.need_inc = False
        self.seq = None
        self.dma = dma
        self.qidx = None
        self.rec_idx = None
        self.label = label
        self.small = False


class Sched:
    def __init__(self):
        self.tasks = []
        self.per_engine = {e: [] for e in ENGINES}
        self.dma_count = {"sync": 0, "gpsimd": 0}
        self.dma_tasks = []

    def add(self, engine, emit, reads=(), writes=(), deps=(), dma=False,
            label="", small=False):
        t = Task(engine, emit, dma, label)
        t.small = small
        t.rec_idx = len(self.tasks)
        dd = set()
        for d in deps:
            if d is not None:
                dd.add(d)
        for r in reads:
            for d in r._absorb():
                dd.add(d)
            if r.writer is not None:
                dd.add(r.writer)
            r.readers.append(t)
        for w in writes:
            for d in w._absorb():
                dd.add(d)
            if w.writer is not None:
                dd.add(w.writer)
            for rd in w.readers:
                dd.add(rd)
            w.writer = t
            w.readers = []
        for d in dd:
            if d is t:
                continue
            assert d.rec_idx < t.rec_idx, (d.label, t.label)
            t.deps.append(d)
        self.tasks.append(t)
        self.per_engine[engine].append(t)
        if dma:
            t.qidx = self.dma_count[engine]
            self.dma_count[engine] += 1
            t.sem_idx = len(self.dma_tasks)
            self.dma_tasks.append(t)
        return t

    def finalize(self, nc, block, sems, dma_sems):
        for t in self.tasks:
            if t.dma:
                t.need_inc = True
            for d in t.deps:
                if d.dma or d.engine != t.engine:
                    d.need_inc = True
        for e in ENGINES:
            seq = 0
            for t in self.per_engine[e]:
                if t.dma:
                    continue
                if t.need_inc:
                    seq += 1
                    t.seq = seq

        sched = self
        trace = {}
        Sched.last_trace = trace

        def make_body(ename):
            def body(eng):
                water = {}
                tr = trace.setdefault(ename, [])
                undrained = set()
                for t in sched.per_engine[ename]:
                    waits = []
                    for d in t.deps:
                        if d.dma:
                            key = ("d", d.sem_idx)
                            val = 16
                            sem = dma_sems[d.sem_idx]
                        elif d.engine == t.engine:
                            continue
                        else:
                            key = ("e", d.engine)
                            val = d.seq
                            sem = sems[d.engine]
                        if water.get(key, 0) >= val:
                            continue
                        water[key] = val
                        waits.append((key, val))
                        eng.wait_ge(sem, val)
                    if ename in ("vector", "scalar"):
                        if any((d.engine == ename and not d.dma
                                and d in undrained) for d in t.deps):
                            eng.drain(fusable=True)
                            undrained.clear()
                    tr.append((t.label, t.seq, waits,
                               [(d.engine, d.label, d.seq if not d.dma else d.qidx)
                                for d in t.deps]))
                    ins = t.emit(eng)
                    if t.dma:
                        ins.then_inc(dma_sems[t.sem_idx], 16)
                    elif t.need_inc:
                        if ename in ("vector", "scalar") and t.small:
                            eng.drain(fusable=True).then_inc(
                                sems[t.engine], 1)
                            undrained.clear()
                        else:
                            ins.then_inc(sems[t.engine], 1)
                    if t.small:
                        undrained.add(t)
            return body

        for ename in ENGINES:
            if not self.per_engine[ename]:
                continue
            body = make_body(ename)
            getattr(block, ename)(body)


def seg512(lo, hi):
    out = []
    a = lo
    while a < hi:
        b = min(hi, (a // 512 + 1) * 512)
        out.append((a, b))
        a = b
    return out


def build_program():
    nc = bass.Bass("TRN2", target_bir_lowering=False,
                   detect_race_conditions=False)

    dx = nc.dram_tensor("x", [S, DM], F32, kind="ExternalInput")
    dxt = nc.dram_tensor("x_t", [DM, S], F32, kind="ExternalInput")
    dwq = nc.dram_tensor("wq_t", [DM, DM], F32, kind="ExternalInput")
    dwk = nc.dram_tensor("wk_t", [DM, DM], F32, kind="ExternalInput")
    dwv = nc.dram_tensor("wv_t", [DM, DM], F32, kind="ExternalInput")
    dwo = nc.dram_tensor("wo_t", [DM, DM], F32, kind="ExternalInput")
    drw = nc.dram_tensor("rw_t", [DM, NE], F32, kind="ExternalInput")
    drb = nc.dram_tensor("rb_rep", [128, NE], F32, kind="ExternalInput")
    dew = nc.dram_tensor("ew_t", [NE, DM, DM], F32, kind="ExternalInput")
    deb = nc.dram_tensor("eb", [NE, DM], F32, kind="ExternalInput")
    didn = nc.dram_tensor("idn_bf", [128, 128], BF16, kind="ExternalInput")
    didnf = nc.dram_tensor("idn_f32", [128, 128], F32, kind="ExternalInput")
    dmask = nc.dram_tensor("maskneg", [128, 128], BF16, kind="ExternalInput")
    dy = nc.dram_tensor("y", [S, DM], F32, kind="ExternalOutput")

    sch = Sched()

    from contextlib import ExitStack
    ctx = ExitStack()

    def sb(name, shape, dt):
        return ctx.enter_context(nc.sbuf_tensor(name, shape, dt))

    def psum(name, shape):
        return ctx.enter_context(nc.psum_tensor(name, shape, F32))

    X32 = sb("X32", [128, NT * DM], F32)
    XT = sb("XT", [128, NDC * S], BF16)
    arA = sb("arA", [128, 32640], BF16)
    ewr = sb("ewr", [128, 16 * DM], BF16)
    vpB = sb("vpB", [128, 2 * NT * 65], BF16)
    h1a = sb("h1a", [128, DM], F32)
    h1b = sb("h1b", [128, DM], F32)
    idn = sb("idn", [128, 128], BF16)
    idnf = sb("idnf", [128, 128], F32)
    maskneg = sb("maskneg_sb", [128, 128], BF16)
    rbrep = sb("rb_rep_sb", [128, NE], F32)
    logitA = sb("logitA", [128, NE], F32)
    logitB = sb("logitB", [128, NE], F32)
    maskw = sb("maskw", [128, NT * NE], F32)
    maskwbf = sb("maskwbf", [128, NE], BF16)
    mwtA = sb("mwtA", [NE, 128], BF16)
    mwtB = sb("mwtB", [NE, 128], BF16)
    recipT = sb("recipT", [128, 8], F32)
    recipTb = sb("recipTb", [128, 8], BF16)
    st_s1 = sb("st_s1", [128, 4], F32)
    st_s2 = sb("st_s2", [128, 4], F32)
    st_nm = sb("st_nm", [128, 4], F32)
    st_var = sb("st_var", [128, 4], F32)
    st_ln = sb("st_ln", [128, 4], F32)
    st_rstd = sb("st_rstd", [128, 4], F32)
    st_nmr = sb("st_nmr", [128, 4], F32)
    mscr = sb("mscr", [128, 12], F32)
    lbm = sb("lbm", [128, 16], F32)
    exd = sb("exd", [128, 16], F32)
    dump8 = sb("dump8", [128, 8], F32)
    wbuf = sb("wbuf", [128, NT], F32)

    attnT = arA[:, 0:16384]
    wosb = arA[:, 16384:24576]
    qpq = arA[:, 24576:26624]
    qpk = arA[:, 26624:28672]
    Ptil = [arA[:, 28672:29696], arA[:, 29696:30720]]
    Rbv = arA[:, 30720:31232]
    sumsb = arA[:, 31232:32256]
    ACC = arA[:, 0:16384].bitcast(F32)
    ebsb = arA[0:NE, 28672:29696]
    outf = [arA[:, 24576:26624].bitcast(F32), arA[:, 26624:28672].bitcast(F32)]

    qpqB = X32[:, 14 * DM:15 * DM].bitcast(BF16)
    qpkB = X32[:, 15 * DM:16 * DM].bitcast(BF16)
    sqdump = vpB[:, 0:DM]

    vp = ewr[:, 0:2 * NT * 65]
    wq_sl = [[ewr[:, 2112 + (3 * b + m) * 1024: 2112 + (3 * b + m + 1) * 1024]
              for m in range(3)] for b in range(2)]

    def ewbuf(b):
        return ewr[:, 8192 * b: 8192 * (b + 1)]

    PS = psum("PS", [128, 4096])

    def bank(k, p0=0, p1=128, lo=0, hi=512):
        return PS[p0:p1, 512 * k + lo:512 * k + hi]

    def bankpair(k, p0=0, p1=128, lo=0, hi=1024):
        return PS[p0:p1, 512 * k + lo:512 * k + hi]

    r_X32 = [Res(f"X32_{t}") for t in range(NT)]
    r_XT = Res("XT")
    r_attnT = Res("attnT")
    r_wo = Res("wo")
    r_qpq = Res("qpq")
    r_qpk = Res("qpk")
    r_qpq2 = Res("qpqB")
    r_qpk2 = Res("qpkB")
    r_vp2 = Res("vpB")
    r_Ptil = [Res("PtilA"), Res("PtilB")]
    r_Rb = Res("Rb")
    r_sums = Res("sums")
    r_vp = Res("vp")
    r_wq = [Res("wqkvA"), Res("wqkvB")]
    r_ew = [Res("ewA"), Res("ewB")]
    r_ACC = [Res(f"ACC{t}", overlays=[r_attnT]) for t in range(8)]
    r_out = [Res("outA", overlays=[r_qpq]), Res("outB", overlays=[r_qpk])]
    r_bank = [Res(f"psb{k}") for k in range(8)]
    r_h1 = [Res("h1a"), Res("h1b")]
    r_logit = [Res("logitA"), Res("logitB")]
    r_maskw = [Res(f"maskw{t}") for t in range(NT)]
    r_wbuf = Res("wbuf")
    r_mwt = [Res("mwtA"), Res("mwtB")]
    r_recip = Res("recip")
    r_stats = [Res(f"stats{i}") for i in range(4)]
    r_XTs = [Res(f"XTs{t}", overlays=[r_XT]) for t in range(NT)]
    r_bstg = [Res("bstgA", overlays=[r_Ptil[1]]), Res("bstgB", overlays=[r_sums])]
    r_consts = Res("consts")
    r_ebsb = Res("ebsb", overlays=[r_Ptil[0]])
    r_sqdump = Res("sqdump", overlays=[r_vp2])
    r_ew[0].overlays = [r_vp, r_wq[0], r_wq[1]]
    r_ew[1].overlays = [r_wq[0], r_wq[1]]
    r_X32[14].overlays = [r_qpq2]
    r_X32[15].overlays = [r_qpk2]

    sch.add("sync", lambda e: e.dma_start(idn[:, :], didn[:, :]),
            writes=[r_consts], dma=True, label="dma_idn")
    sch.add("sync", lambda e: e.dma_start(idnf[:, :], didnf[:, :]),
            writes=[r_consts], dma=True, label="dma_idnf")
    sch.add("sync", lambda e: e.dma_start(maskneg[:, :], dmask[:, :]),
            writes=[r_consts], dma=True, label="dma_mask")
    sch.add("sync", lambda e: e.dma_start(rbrep[:, :], drb[:, :]),
            writes=[r_consts], dma=True, label="dma_rb")
    sch.add("sync",
            lambda e: e.dma_start(
                X32[:, 0:14 * DM].rearrange("p (t n) -> p t n", t=14),
                dx[0:14 * 128, :].rearrange("(t p) n -> p t n", p=128)),
            writes=list(r_X32[:14]), dma=True, label="dma_x32")
    sch.add("gpsimd",
            lambda e: e.dma_start(
                XT[:, :].rearrange("p (c n) -> p c n", c=NDC),
                dxt[:, :].rearrange("(c p) n -> p c n", p=128)),
            writes=[r_XT], dma=True, label="dma_xt")
    sch.add("gpsimd",
            lambda e: e.dma_start(
                wosb[:, :].rearrange("p (c n) -> p c n", c=NDC),
                dwo[:, :].rearrange("(c p) n -> p c n", p=128)),
            writes=[r_wo], dma=True, label="dma_wo")


    def mk_vones(buf, rr, m, t):
        def f(e):
            return e.memset(buf[:, (m * NT + t) * 65 + 64:
                                (m * NT + t) * 65 + 65], 1.0)
        return f
    for m in range(2):
        for t in range(NT):
            sch.add("vector", mk_vones(vp, r_vp, m, t), writes=[r_vp],
                    label="vones")
            sch.add("vector", mk_vones(vpB, r_vp2, m, t), writes=[r_vp2],
                    label="vonesB")

    wmats = [dwq, dwk, dwv]

    def mk_wdma(p, m):
        buf = wq_sl[p % 2][m]

        def f(e):
            src = wmats[m][:, 128 * p:128 * (p + 1)]
            return e.dma_start(
                buf.rearrange("p (c n) -> p c n", c=NDC),
                src.rearrange("(c p) n -> p c n", p=128))
        return f

    def rec_wdma(p):
        for m in range(3):
            sch.add("gpsimd", mk_wdma(p, m), writes=[r_wq[p % 2]],
                    dma=True, label=f"dma_w{p}{m}")

    rec_wdma(0)
    rec_wdma(1)

    rw8 = sb("rw8", [128, NDC * NE], BF16)
    r_rw8 = Res("rw8")
    sch.add("gpsimd",
            lambda e: e.dma_start(
                rw8[:, :].rearrange("p (c n) -> p c n", c=NDC),
                drw[:, :].rearrange("(c p) n -> p c n", p=128)),
            writes=[r_rw8], dma=True, label="dma_rw8")

    filler = []

    qp_sets = [(qpq, qpk, r_qpq, r_qpk), (qpqB, qpkB, r_qpq2, r_qpk2)]
    vp_sets = [(vp, r_vp), (vpB, r_vp2)]

    def queue_proj(p):
        wb = r_wq[p % 2]
        wsl = wq_sl[p % 2]
        qpq_, qpk_, r_qpq_, r_qpk_ = qp_sets[p % 2]
        vp_, r_vp_ = vp_sets[p % 2]
        items = []

        def qk_item(m, seg, dstap, rres, fb):
            def rec():
                def mmf(e):
                    last = None
                    for c in range(NDC):
                        last = e.matmul(
                            bank(fb, lo=0, hi=256),
                            wsl[m][:, 128 * c:128 * (c + 1)],
                            XT[:, 2048 * c + 256 * seg:
                               2048 * c + 256 * (seg + 1)],
                            start=(c == 0), stop=(c == NDC - 1))
                    return last
                sch.add("tensor", mmf, reads=[wb, r_XT], writes=[r_bank[fb]],
                        label=f"fproj{p}_{m}_{seg}")

                def evf(e):
                    return e.tensor_copy(
                        dstap[:, 256 * seg:256 * (seg + 1)],
                        bank(fb, lo=0, hi=256))
                sch.add("vector", evf, reads=[r_bank[fb]], writes=[rres],
                        label=f"fev{p}_{m}_{seg}")
            return rec

        def v_item(g, ti, fb):
            def rec():
                t = 4 * g + ti

                def mmf(e):
                    last = None
                    for c in range(NDC):
                        last = e.matmul(
                            bank(fb, lo=128 * ti, hi=128 * (ti + 1)),
                            XT[:, 2048 * c + 128 * t:
                               2048 * c + 128 * (t + 1)],
                            wsl[2][:, 128 * c:128 * (c + 1)],
                            start=(c == 0), stop=(c == NDC - 1))
                    return last

                def evf(e):
                    last = None
                    for m in range(2):
                        last = e.tensor_copy(
                            vp_[:, (m * NT + t) * 65:
                                (m * NT + t) * 65 + 64],
                            bank(fb, lo=128 * ti + 64 * m,
                                 hi=128 * ti + 64 * (m + 1)))
                    return last
                sch.add("tensor", mmf, reads=[wb, r_XT], writes=[r_bank[fb]],
                        label=f"fprojv{p}_{g}_{ti}")
                sch.add("vector", evf, reads=[r_bank[fb]], writes=[r_vp_],
                        label=f"fevv{p}_{g}_{ti}")
            return rec

        k = 0
        for m, dstap, rres in ((0, qpq_, r_qpq_), (1, qpk_, r_qpk_)):
            for seg in range(8):
                items.append(qk_item(m, seg, dstap, rres, 6 + (k % 2)))
                k += 1
        for g in range(4):
            for ti in range(4):
                items.append(v_item(g, ti, 6 + (k % 2)))
                k += 1
        return items

    def pop_filler(n=1):
        for _ in range(n):
            if filler:
                filler.pop(0)()

    pending_ep = []

    def flush_ep(n=-1):
        while pending_ep and n != 0:
            pending_ep.pop(0)()
            n -= 1

    def attn_head_quarter(p, m, qq, obuf):
        flush_ep(len(pending_ep) - 2)
        h = 2 * p + m
        qh0 = 512 * qq
        jmax = 4 * qq + 3
        hb = 64 * m
        ob = 4 + obuf
        qpq_, qpk_, r_qpq_, r_qpk_ = qp_sets[p % 2]
        vp_, r_vp_ = vp_sets[p % 2]

        def los(j):
            return max(qh0, 128 * j) - qh0

        def mk_S(j, sb_):
            lo = los(j)
            hasdiag = (j >= 4 * qq)

            def f(e):
                last = e.matmul(
                    bank(sb_, lo=lo, hi=512),
                    qpk_[hb:hb + 64, 128 * j:128 * (j + 1)],
                    qpq_[hb:hb + 64, qh0 + lo:qh0 + 512],
                    start=True, stop=not hasdiag)
                if hasdiag:
                    last = e.matmul(bank(sb_, lo=lo, hi=lo + 128), idn[:, :],
                                    maskneg[:, :], start=False, stop=True)
                return last
            return f

        def mk_exp(je, pb):
            lo = los(je)

            def f(e):
                return e.activation(Ptil[pb][:, lo:1024],
                                    PS[:, 1024 * pb + lo:1024 * pb + 1024],
                                    AF.Exp, bias=np.float32(-CSOFT),
                                    scale=np.float32(SMSCALE))
            return f

        def mk_PV(j, pb):
            lo = los(j)
            off = 512 * (j % 2)

            def f(e):
                return e.matmul(
                    bank(ob, p1=65, lo=lo, hi=512),
                    vp_[:, (m * NT + j) * 65:(m * NT + j + 1) * 65],
                    Ptil[pb][:, off + lo:off + 512],
                    start=(j == 0), stop=(j == jmax))
            return f

        npairs = (jmax + 1) // 2
        for pi in range(npairs):
            je, jo = 2 * pi, 2 * pi + 1
            pb = pi % 2
            sch.add("tensor", mk_S(je, 2 * pb), reads=[r_qpq_, r_qpk_, r_consts],
                    writes=[r_bank[2 * pb]], label=f"S{h}_{qq}_{je}")
            sch.add("tensor", mk_S(jo, 2 * pb + 1),
                    reads=[r_qpq_, r_qpk_, r_consts],
                    writes=[r_bank[2 * pb + 1]], label=f"S{h}_{qq}_{jo}")
            sch.add("scalar", mk_exp(je, pb),
                    reads=[r_bank[2 * pb], r_bank[2 * pb + 1]],
                    writes=[r_Ptil[pb]], label=f"exp{h}_{qq}_{pi}")
            if pi >= 1:
                sch.add("tensor", mk_PV(2 * pi - 2, (pi - 1) % 2),
                        reads=[r_Ptil[(pi - 1) % 2], r_vp_],
                        writes=[r_bank[ob]], label=f"PV{h}_{qq}_{2*pi-2}")
                sch.add("tensor", mk_PV(2 * pi - 1, (pi - 1) % 2),
                        reads=[r_Ptil[(pi - 1) % 2], r_vp_],
                        writes=[r_bank[ob]], label=f"PV{h}_{qq}_{2*pi-1}")
            if pi >= 1:
                flush_ep(1)
            pop_filler(1)
        pb = (npairs - 1) % 2
        sch.add("tensor", mk_PV(jmax - 1, pb), reads=[r_Ptil[pb], r_vp_],
                writes=[r_bank[ob]], label=f"PV{h}_{qq}_{jmax-1}")
        sch.add("tensor", mk_PV(jmax, pb), reads=[r_Ptil[pb], r_vp_],
                writes=[r_bank[ob]], label=f"PV{h}_{qq}_{jmax}")
        def ep1():
            def f_sums(e):
                return e.copy(sumsb[0:1, 0:512], bank(ob, p0=64, p1=65))
            sch.add("scalar", f_sums, reads=[r_bank[ob]], writes=[r_sums],
                    label=f"sums{h}_{qq}", small=True)

            def f_T(e):
                last = None
                for i in range(4):
                    last = e.matmul(bank(6, lo=i, hi=i + 1),
                                    sumsb[0:1, 128 * i:128 * (i + 1)],
                                    idn[0:1, 0:1], start=True, stop=True)
                return last
            sch.add("tensor", f_T, reads=[r_sums, r_consts],
                    writes=[r_bank[6]], label=f"epT{h}_{qq}")

            def f_recip(e):
                pass
        def ep2():
            def f_recip(e):
                with nc.allow_low_precision(reason="bf16 softmax recip"):
                    return e.reciprocal(recipTb[:, 0:4], bank(6, hi=4))
            sch.add("vector", f_recip, reads=[r_bank[6]], writes=[r_recip],
                    label=f"recip{h}_{qq}", small=True)

            def f_bc(e):
                last = None
                for i in range(4):
                    last = e.matmul(
                        bank(7, p1=64, lo=128 * i, hi=128 * (i + 1)),
                        recipTb[:, i:i + 1].broadcast_to([128, 64]),
                        idn[:, :], start=True, stop=True)
                return last
            sch.add("tensor", f_bc, reads=[r_recip, r_consts],
                    writes=[r_bank[7]], label=f"epbc{h}_{qq}")

            def f_evR(e):
                return e.tensor_copy(Rbv[0:64, 0:512], bank(7, p1=64))
            sch.add("vector", f_evR, reads=[r_bank[7]], writes=[r_Rb],
                    label=f"evR{h}_{qq}")

            ch = h // 2

            def f_mul(e):
                return e.tensor_tensor(
                    attnT[hb:hb + 64, 2048 * ch + qh0:2048 * ch + qh0 + 512],
                    bank(ob, p1=64), Rbv[0:64, 0:512], OP.mult)
            sch.add("vector", f_mul, reads=[r_bank[ob], r_Rb],
                    writes=[r_attnT], label=f"emul{h}_{qq}")
        pending_ep.append(ep1)
        pending_ep.append(ep2)

    obuf = 0
    for p in range(NH // 2):
        if p == 0:
            for it in queue_proj(0):
                it()
        if p + 1 < NH // 2:
            filler.extend(queue_proj(p + 1))
        if p + 2 < NH // 2:
            rec_wdma(p + 2)
        for m in range(2):
            for qq in range(4):
                attn_head_quarter(p, m, qq, obuf)
                obuf ^= 1
        while filler:
            pop_filler(1)
    flush_ep()

    for t in (14, 15):
        def f_xlate(t=t):
            def f(e):
                return e.dma_start(X32[:, DM * t:DM * (t + 1)],
                                   dx[128 * t:128 * (t + 1), :])
            return f
        sch.add("sync", f_xlate(), writes=[r_X32[t]], dma=True,
                label=f"dma_x32_{t}")

    pool = [(bankpair(0), [r_bank[0], r_bank[1]]),
            (bankpair(2), [r_bank[2], r_bank[3]]),
            (bankpair(4), [r_bank[4], r_bank[5]])]
    pooli = [0]

    def take_pool():
        ps, rr = pool[pooli[0] % 3]
        pooli[0] += 1
        return ps, rr

    LNEPS = 1e-5

    def ln_tasks(t, src_ps, rsrc_ps, sset, phase):
        hb = h1a if (t % 2 == 0) else h1b
        rhb = r_h1[t % 2]
        sl = slice(2 * 0 + sset, 2 * 0 + sset + 1)
        is2 = (phase == 2)
        accslot = t % 8

        def f_add(e):
            if is2:
                return e.scalar_tensor_tensor(
                    hb[:, :], ACC[:, DM * accslot:DM * (accslot + 1)],
                    wbuf[:, t:t + 1],
                    X32[:, DM * t:DM * (t + 1)], OP.mult, OP.add,
                    accum_out=st_s1[:, sset:sset + 1])
            return e.scalar_tensor_tensor(
                hb[:, :], src_ps[:, 0:1024], 1.0,
                X32[:, DM * t:DM * (t + 1)], OP.mult, OP.add,
                accum_out=st_s1[:, sset:sset + 1])
        reads = ([r_ACC[accslot]] if is2 else list(rsrc_ps)) + [r_X32[t]]
        if is2:
            reads.append(r_wbuf)
        sch.add("vector", f_add, reads=reads, writes=[rhb, r_stats[sset]],
                label=f"lnadd{phase}_{t}", small=True)

        def f_sq(e):
            return e.scalar_tensor_tensor(
                sqdump[:, :], hb[:, :], 1.0, hb[:, :], OP.mult, OP.mult,
                accum_out=st_s2[:, sset:sset + 1])
        sch.add("vector", f_sq, reads=[rhb],
                writes=[r_stats[sset], r_sqdump],
                label=f"lnsq{phase}_{t}", small=True)

        def f_nm(e):
            return e.activation(st_nm[:, sset:sset + 1],
                                st_s1[:, sset:sset + 1], AF.Copy,
                                scale=np.float32(-1.0 / DM))
        sch.add("scalar", f_nm, reads=[r_stats[sset]], writes=[r_stats[sset]],
                label=f"lnnm{phase}_{t}", small=True)

        def f_e2(e):
            return e.activation(st_var[:, sset:sset + 1],
                                st_s2[:, sset:sset + 1], AF.Copy,
                                scale=np.float32(1.0 / DM))
        sch.add("scalar", f_e2, reads=[r_stats[sset]], writes=[r_stats[sset]],
                label=f"lne2{phase}_{t}", small=True)

        def f_m2(e):
            return e.scalar_tensor_tensor(
                st_var[:, sset:sset + 1], st_nm[:, sset:sset + 1],
                st_nm[:, sset:sset + 1], st_var[:, sset:sset + 1],
                OP.mult, OP.subtract)
        sch.add("vector", f_m2, reads=[r_stats[sset]],
                writes=[r_stats[sset]], label=f"lnm2{phase}_{t}", small=True)

        def f_ln(e):
            return e.activation(st_ln[:, sset:sset + 1],
                                st_var[:, sset:sset + 1], AF.Ln,
                                bias=np.float32(LNEPS),
                                scale=np.float32(-1.0))
        sch.add("scalar", f_ln, reads=[r_stats[sset]], writes=[r_stats[sset]],
                label=f"lnln{phase}_{t}", small=True)

        def f_rstd(e):
            return e.activation(st_rstd[:, sset:sset + 1],
                                st_ln[:, sset:sset + 1], AF.Exp,
                                scale=np.float32(-0.5))
        sch.add("scalar", f_rstd, reads=[r_stats[sset]], writes=[r_stats[sset]],
                label=f"lnrstd{phase}_{t}", small=True)

        def f_nmr(e):
            return e.tensor_scalar(st_nmr[:, sset:sset + 1],
                                   st_nm[:, sset:sset + 1],
                                   st_rstd[:, sset:sset + 1], None, OP.mult)
        sch.add("vector", f_nmr, reads=[r_stats[sset]], writes=[r_stats[sset]],
                label=f"lnnmr{phase}_{t}", small=True)

        if not is2:
            def f_norm(e):
                return e.activation(X32[:, DM * t:DM * (t + 1)], hb[:, :],
                                    AF.Identity, bias=st_nmr[:, sset:sset + 1],
                                    scale=st_rstd[:, sset:sset + 1])
            sch.add("scalar", f_norm, reads=[rhb, r_stats[sset]],
                    writes=[r_X32[t]], label=f"lnnorm{phase}_{t}")
        else:
            ob = t % 2

            def f_norm(e):
                return e.activation(outf[ob][:, :], hb[:, :],
                                    AF.Identity, bias=st_nmr[:, sset:sset + 1],
                                    scale=st_rstd[:, sset:sset + 1])
            sch.add("scalar", f_norm, reads=[rhb, r_stats[sset]],
                    writes=[r_out[ob]], label=f"lnnorm{phase}_{t}")

            def f_st(e):
                return e.dma_start(dy[128 * t:128 * (t + 1), :],
                                   outf[ob][:, :])
            sch.add("sync", f_st, reads=[r_out[ob]], dma=True,
                    label=f"store{t}")

    def rec_oln(t):
        ps, rps = take_pool()

        def f_oproj(t=t, ps=ps):
            def f(e):
                last = None
                for c in range(NDC):
                    for sg in range(2):
                        last = e.matmul(
                            ps[:, 512 * sg:512 * (sg + 1)],
                            attnT[:, 2048 * c + 128 * t:2048 * c + 128 * (t + 1)],
                            wosb[:, 1024 * c + 512 * sg:1024 * c + 512 * (sg + 1)],
                            start=(c == 0), stop=(c == NDC - 1))
                return last
            return f
        sch.add("tensor", f_oproj(), reads=[r_attnT, r_wo], writes=list(rps),
                label=f"oproj{t}")
        ln_tasks(t, ps, rps, t % 2, phase=1)

    def rec_dphase(t):
        ps, rps = take_pool()

        def f_xt(t=t, ps=ps):
            def f(e):
                last = None
                for c in range(NDC):
                    last = e.matmul(
                        ps[:, 128 * c:128 * (c + 1)],
                        X32[:, DM * t + 128 * c:DM * t + 128 * (c + 1)],
                        idnf[:, :], start=True, stop=True)
                return last
            return f
        sch.add("tensor", f_xt(), reads=[r_X32[t], r_consts], writes=list(rps),
                label=f"xmt{t}")

        def f_evxt(t=t, ps=ps):
            def f(e):
                return e.activation(
                    XT[:, :].rearrange("p (c n) -> p c n", c=NDC)[:, :, 128 * t:128 * (t + 1)],
                    ps[:, 0:1024].rearrange("p (c n) -> p c n", c=NDC),
                    AF.Copy)
            return f
        sch.add("scalar", f_evxt(), reads=list(rps), writes=[r_XTs[t]],
                label=f"evxmt{t}")

        lb = logitA if t % 2 == 0 else logitB
        rlb = r_logit[t % 2]

        rbk = 6 + (t % 2)

        def f_rmm(t=t, rbk=rbk):
            def f(e):
                last = None
                for c in range(NDC):
                    last = e.matmul(
                        bank(rbk, hi=NE),
                        XT[:, 2048 * c + 128 * t:2048 * c + 128 * (t + 1)],
                        rw8[:, NE * c:NE * (c + 1)],
                        start=(c == 0), stop=(c == NDC - 1))
                return last
            return f
        sch.add("tensor", f_rmm(), reads=[r_XTs[t], r_rw8],
                writes=[r_bank[rbk]], label=f"rmm{t}")

        def f_rev(lb=lb, rbk=rbk):
            def f(e):
                return e.tensor_add(lb[:, :], bank(rbk, hi=NE), rbrep[:, :])
            return f
        sch.add("vector", f_rev(), reads=[r_bank[rbk], r_consts], writes=[rlb],
                label=f"rev{t}", small=True)

        s_ = t % 2
        sr = 2 + s_

        def f_rtree(lb=lb, s_=s_, sr=sr):
            def f(e):
                e.tensor_max(mscr[:, 6 * s_:6 * s_ + 4], lb[:, 0:4], lb[:, 4:8])
                e.drain(fusable=True)
                e.tensor_max(mscr[:, 6 * s_ + 4:6 * s_ + 6],
                             mscr[:, 6 * s_:6 * s_ + 2],
                             mscr[:, 6 * s_ + 2:6 * s_ + 4])
                e.drain(fusable=True)
                return e.tensor_max(st_s2[:, sr:sr + 1],
                                    mscr[:, 6 * s_ + 4:6 * s_ + 5],
                                    mscr[:, 6 * s_ + 5:6 * s_ + 6])
            return f
        sch.add("vector", f_rtree(), reads=[rlb], writes=[r_stats[sr]],
                label=f"rtree{t}", small=True)

        def f_rsub(lb=lb, s_=s_, sr=sr):
            def f(e):
                return e.tensor_scalar(lbm[:, 8 * s_:8 * s_ + 8], lb[:, :],
                                       st_s2[:, sr:sr + 1], None, OP.subtract)
            return f
        sch.add("vector", f_rsub(), reads=[rlb, r_stats[sr]],
                writes=[r_stats[sr]], label=f"rsub{t}", small=True)

        def f_rexp(s_=s_):
            def f(e):
                return e.activation(exd[:, 8 * s_:8 * s_ + 8],
                                    lbm[:, 8 * s_:8 * s_ + 8], AF.Exp)
            return f
        sch.add("scalar", f_rexp(), reads=[r_stats[sr]], writes=[r_stats[sr]],
                label=f"rexp{t}", small=True)

        def f_rsum(s_=s_, sr=sr):
            def f(e):
                return e.scalar_tensor_tensor(
                    dump8[:, :], exd[:, 8 * s_:8 * s_ + 8], 1.0,
                    exd[:, 8 * s_:8 * s_ + 8], OP.mult, OP.bypass,
                    accum_out=st_var[:, sr:sr + 1])
            return f
        sch.add("vector", f_rsum(), reads=[r_stats[sr]], writes=[r_stats[sr]],
                label=f"rsum{t}", small=True)

        def f_rmask(s_=s_, t=t):
            def f(e):
                return e.tensor_single_scalar(maskw[:, NE * t:NE * (t + 1)],
                                              lbm[:, 8 * s_:8 * s_ + 8],
                                              0.0, OP.is_ge)
            return f
        sch.add("vector", f_rmask(), reads=[r_stats[sr]],
                writes=[r_maskw[t]], label=f"rmask{t}", small=True)

        def f_wrecip(sr=sr, t=t):
            def f(e):
                return e.reciprocal(wbuf[:, t:t + 1], st_var[:, sr:sr + 1])
            return f
        sch.add("vector", f_wrecip(), reads=[r_stats[sr]], writes=[r_wbuf],
                label=f"wrecip{t}", small=True)

    def rec_ewdma(li, ei):
        b = li % 2

        def f(e):
            return e.dma_start(
                ewbuf(b).rearrange("p (c n) -> p c n", c=NDC),
                dew[ei].rearrange("(c p) n -> p c n", p=128))
        sch.add("gpsimd", f, writes=[r_ew[b]], dma=True,
                label=f"dma_ew{li}")

    def rec_mcast(t):
        rmt = r_mwt[t % 2]

        def f_mcast(e):
            return e.activation(maskwbf[:, :], maskw[:, NE * t:NE * (t + 1)],
                                AF.Copy)
        sch.add("scalar", f_mcast, reads=[r_maskw[t]], writes=[rmt],
                label=f"mcast{t}", small=True)

    def rec_tile_init(t, li):
        rec_mcast(t)
        mt = mwtA if t % 2 == 0 else mwtB
        rmt = r_mwt[t % 2]
        accslot = t % 8
        mbk = 6 + (t % 2)
        eb_ = li % 2

        ps_e, rps_e = take_pool()

        def f_emm(t=t, ps=ps_e, eb_=eb_):
            def f(e):
                last = None
                buf = ewbuf(eb_)
                for c in range(NDC):
                    for sg in range(2):
                        last = e.matmul(
                            ps[:, 512 * sg:512 * (sg + 1)],
                            XT[:, 2048 * c + 128 * t:
                               2048 * c + 128 * (t + 1)],
                            buf[:, 1024 * c + 512 * sg:
                                1024 * c + 512 * (sg + 1)],
                            start=(c == 0), stop=(c == NDC - 1))
                return last
            return f
        sch.add("tensor", f_emm(), reads=[r_XTs[t], r_ew[eb_]],
                writes=list(rps_e), label=f"emm{li}_{t}")

        def f_mT(e, mbk=mbk):
            return e.matmul(bank(mbk, p1=NE, lo=256, hi=384), maskwbf[:, :],
                            idn[:, :], start=True, stop=True)
        sch.add("tensor", f_mT, reads=[rmt, r_consts], writes=[r_bank[mbk]],
                label=f"mT{t}")

        def f_mev(e, mbk=mbk):
            return e.activation(mt[:, :], bank(mbk, p1=NE, lo=256, hi=384),
                                AF.Copy)
        sch.add("scalar", f_mev, reads=[r_bank[mbk]], writes=[rmt],
                label=f"mev{t}", small=True)

        ps_b, rps_b = take_pool()

        def f_bias(ps=ps_b):
            def f(e):
                last = None
                for sg in range(2):
                    last = e.matmul(ps[:, 512 * sg:512 * (sg + 1)],
                                    mt[:, :],
                                    ebsb[:, 512 * sg:512 * (sg + 1)],
                                    start=True, stop=True)
                return last
            return f
        sch.add("tensor", f_bias(), reads=[rmt, r_ebsb], writes=list(rps_b),
                label=f"bias{t}")

        bstg = arA[:, 29696:30720] if t % 2 == 0 else arA[:, 31232:32256]
        rbs = r_bstg[t % 2]

        def f_bst(e, ps=ps_b, bstg=bstg):
            return e.activation(bstg, ps[:, 0:1024], AF.Copy)
        sch.add("scalar", f_bst, reads=list(rps_b), writes=[rbs],
                label=f"bstg{t}")

        def f_estt(t=t, pe=ps_e, bstg=bstg, accslot=accslot):
            def f(e):
                return e.scalar_tensor_tensor(
                    ACC[:, DM * accslot:DM * (accslot + 1)],
                    pe[:, 0:1024],
                    maskw[:, NE * t:NE * t + 1],
                    bstg,
                    OP.mult, OP.add)
            return f
        sch.add("vector", f_estt(),
                reads=list(rps_e) + [rbs, r_maskw[t]],
                writes=[r_ACC[accslot]], label=f"estt{li}_{t}")

    def rec_expert_tile(li, e_i, t, init=False):
        eb_ = li % 2
        ps, rps = take_pool()
        accslot = t % 8

        def f_emm(t=t, ps=ps, eb_=eb_):
            def f(e):
                last = None
                buf = ewbuf(eb_)
                for c in range(NDC):
                    for sg in range(2):
                        last = e.matmul(
                            ps[:, 512 * sg:512 * (sg + 1)],
                            XT[:, 2048 * c + 128 * t:
                               2048 * c + 128 * (t + 1)],
                            buf[:, 1024 * c + 512 * sg:
                                1024 * c + 512 * (sg + 1)],
                            start=(c == 0), stop=(c == NDC - 1))
                return last
            return f
        sch.add("tensor", f_emm(),
                reads=[r_XTs[t], r_ew[eb_]],
                writes=list(rps), label=f"emm{li}_{t}")

        def f_estt(t=t, ps=ps, e_i=e_i, accslot=accslot, init=init):
            def f(e):
                return e.scalar_tensor_tensor(
                    ACC[:, DM * accslot:DM * (accslot + 1)],
                    ps[:, 0:1024],
                    maskw[:, NE * t + e_i:NE * t + e_i + 1],
                    ACC[:, DM * accslot:DM * (accslot + 1)],
                    OP.mult, OP.bypass if init else OP.add)
            return f
        rdd = list(rps) + [r_maskw[t]]
        if not init:
            rdd.append(r_ACC[accslot])
        sch.add("vector", f_estt(), reads=rdd,
                writes=[r_ACC[accslot]], label=f"estt{li}_{t}")

    for t in range(NT):
        rec_oln(t)
    sch.add("gpsimd", lambda e: e.dma_start(ebsb, deb[:, :]),
            writes=[r_ebsb], dma=True, label="dma_eb")
    rec_ewdma(0, 0)
    for t in range(8):
        rec_dphase(t)
        rec_tile_init(t, 0)
    rec_ewdma(1, 1)
    for e_i in range(1, NE):
        li = e_i
        rec_dphase(7 + e_i)
        for t in range(8):
            rec_expert_tile(li, e_i, t)
            if e_i == NE - 1 and t == 3:
                rec_dphase(15)
        rec_ewdma(li + 1, (li + 1) % NE)
    for t in range(8, 16):
        ln_tasks(t - 8, None, None, (t - 8) % 2, phase=2)
        rec_tile_init(t, 8)
    rec_ewdma(9, 1)
    for e_i in range(1, NE):
        li = 8 + e_i
        for t in range(8, 16):
            rec_expert_tile(li, e_i, t)
            if e_i == NE - 1:
                ln_tasks(t, None, None, t % 2, phase=2)
        if li + 1 < 16:
            rec_ewdma(li + 1, (li + 1) % NE)

    store_tasks = [t for t in sch.per_engine["sync"] if t.label.startswith("store")]

    def f_fin(e):
        return None
    fin = Task("sync", f_fin, False, "final")
    fin.rec_idx = len(sch.tasks)
    fin.deps = store_tasks
    sch.tasks.append(fin)
    sch.per_engine["sync"].append(fin)

    with ctx:
        sem_t = ctx.enter_context(nc.semaphore("sem_t"))
        sem_v = ctx.enter_context(nc.semaphore("sem_v"))
        sem_s = ctx.enter_context(nc.semaphore("sem_s"))
        sem_g = ctx.enter_context(nc.semaphore("sem_g"))
        sem_sp = ctx.enter_context(nc.semaphore("sem_sp"))
        dma_sems = [ctx.enter_context(nc.semaphore(f"dq{i}"))
                    for i in range(len(sch.dma_tasks))]
        with nc.Block() as block:
            sems = {"tensor": sem_t, "vector": sem_v, "scalar": sem_s,
                    "sync": sem_sp, "gpsimd": sem_g}
            sch.finalize(nc, block, sems, dma_sems)
    return nc


_NC_CACHE = None


def kernel(L0_attn_x, L0_attn_W_Q, L0_attn_W_K, L0_attn_W_V, L0_attn_W_O,
           L0_ffn_router_w, L0_ffn_router_b, L0_ffn_expert_w, L0_ffn_expert_b):
    global _NC_CACHE
    x = np.ascontiguousarray(np.asarray(L0_attn_x, dtype=np.float32))
    wq_t = np.ascontiguousarray(np.asarray(L0_attn_W_Q, np.float32).T)
    wk_t = np.ascontiguousarray(np.asarray(L0_attn_W_K, np.float32).T)
    wv_t = np.ascontiguousarray(np.asarray(L0_attn_W_V, np.float32).T)
    wo_t = np.ascontiguousarray(np.asarray(L0_attn_W_O, np.float32).T)
    rw_t = np.ascontiguousarray(np.asarray(L0_ffn_router_w, np.float32).T)
    rb_rep = np.ascontiguousarray(
        np.broadcast_to(np.asarray(L0_ffn_router_b, np.float32), (128, NE)))
    ew_t = np.ascontiguousarray(
        np.asarray(L0_ffn_expert_w, np.float32).transpose(0, 2, 1))
    eb = np.ascontiguousarray(np.asarray(L0_ffn_expert_b, np.float32))

    idn_bf = np.eye(128, dtype=BFNP)
    idn_f32 = np.eye(128, dtype=np.float32)
    kk, qq = np.meshgrid(np.arange(128), np.arange(128), indexing="ij")
    maskneg = np.where(qq >= kk, 0.0, -1e9).astype(BFNP)

    if _NC_CACHE is None:
        _NC_CACHE = build_program()
    nc = _NC_CACHE

    in_maps = []
    for b in range(8):
        xb = np.ascontiguousarray(x[b])
        in_maps.append({
            "x": xb,
            "x_t": np.ascontiguousarray(xb.T),
            "wq_t": wq_t, "wk_t": wk_t, "wv_t": wv_t, "wo_t": wo_t,
            "rw_t": rw_t, "rb_rep": rb_rep, "ew_t": ew_t, "eb": eb,
            "idn_bf": idn_bf, "idn_f32": idn_f32, "maskneg": maskneg,
        })
    res = run_bass_kernel_spmd(nc, in_maps, core_ids=list(range(8)))
    out = np.stack([np.asarray(res.results[b]["y"], dtype=np.float32)
                    for b in range(8)], axis=0)
    return out


# version 32
# speedup vs baseline: 1.0051x; 1.0051x over previous
import numpy as np
import ml_dtypes

import concourse.bass as bass
import concourse.mybir as mybir
from concourse.bass_utils import run_bass_kernel_spmd

F32 = mybir.dt.float32
BF16 = mybir.dt.bfloat16
AF = mybir.ActivationFunctionType
OP = mybir.AluOpType
BFNP = ml_dtypes.bfloat16

S = 2048
DM = 1024
NH = 16
DH = 64
NE = 8
NT = 16
NDC = 8
CSOFT = 15.0
SMSCALE = 0.125

ENGINES = ("tensor", "vector", "scalar", "sync", "gpsimd")


class Res:
    __slots__ = ("name", "writer", "readers", "overlays", "absorbed")

    def __init__(self, name, overlays=()):
        self.name = name
        self.writer = None
        self.readers = []
        self.overlays = list(overlays)
        self.absorbed = False

    def _absorb(self):
        if self.absorbed or not self.overlays:
            return []
        self.absorbed = True
        deps = []
        for o in self.overlays:
            if o.writer is not None:
                deps.append(o.writer)
            deps.extend(o.readers)
        return deps


class Task:
    __slots__ = ("engine", "emit", "deps", "need_inc", "seq", "dma", "qidx",
                 "rec_idx", "label", "sem_idx", "small")

    def __init__(self, engine, emit, dma, label):
        self.engine = engine
        self.emit = emit
        self.deps = []
        self.need_inc = False
        self.seq = None
        self.dma = dma
        self.qidx = None
        self.rec_idx = None
        self.label = label
        self.small = False


class Sched:
    def __init__(self):
        self.tasks = []
        self.per_engine = {e: [] for e in ENGINES}
        self.dma_count = {"sync": 0, "gpsimd": 0}
        self.dma_tasks = []

    def add(self, engine, emit, reads=(), writes=(), deps=(), dma=False,
            label="", small=False):
        t = Task(engine, emit, dma, label)
        t.small = small
        t.rec_idx = len(self.tasks)
        dd = set()
        for d in deps:
            if d is not None:
                dd.add(d)
        for r in reads:
            for d in r._absorb():
                dd.add(d)
            if r.writer is not None:
                dd.add(r.writer)
            r.readers.append(t)
        for w in writes:
            for d in w._absorb():
                dd.add(d)
            if w.writer is not None:
                dd.add(w.writer)
            for rd in w.readers:
                dd.add(rd)
            w.writer = t
            w.readers = []
        for d in dd:
            if d is t:
                continue
            assert d.rec_idx < t.rec_idx, (d.label, t.label)
            t.deps.append(d)
        self.tasks.append(t)
        self.per_engine[engine].append(t)
        if dma:
            t.qidx = self.dma_count[engine]
            self.dma_count[engine] += 1
            t.sem_idx = len(self.dma_tasks)
            self.dma_tasks.append(t)
        return t

    def finalize(self, nc, block, sems, dma_sems):
        for t in self.tasks:
            if t.dma:
                t.need_inc = True
            for d in t.deps:
                if d.dma or d.engine != t.engine:
                    d.need_inc = True
        for e in ENGINES:
            seq = 0
            for t in self.per_engine[e]:
                if t.dma:
                    continue
                if t.need_inc:
                    seq += 1
                    t.seq = seq

        sched = self
        trace = {}
        Sched.last_trace = trace

        def make_body(ename):
            def body(eng):
                water = {}
                tr = trace.setdefault(ename, [])
                undrained = set()
                for t in sched.per_engine[ename]:
                    waits = []
                    for d in t.deps:
                        if d.dma:
                            key = ("d", d.sem_idx)
                            val = 16
                            sem = dma_sems[d.sem_idx]
                        elif d.engine == t.engine:
                            continue
                        else:
                            key = ("e", d.engine)
                            val = d.seq
                            sem = sems[d.engine]
                        if water.get(key, 0) >= val:
                            continue
                        water[key] = val
                        waits.append((key, val))
                        eng.wait_ge(sem, val)
                    if ename in ("vector", "scalar"):
                        if any((d.engine == ename and not d.dma
                                and d in undrained) for d in t.deps):
                            eng.drain(fusable=True)
                            undrained.clear()
                    tr.append((t.label, t.seq, waits,
                               [(d.engine, d.label, d.seq if not d.dma else d.qidx)
                                for d in t.deps]))
                    ins = t.emit(eng)
                    if t.dma:
                        ins.then_inc(dma_sems[t.sem_idx], 16)
                    elif t.need_inc:
                        if ename in ("vector", "scalar") and t.small:
                            eng.drain(fusable=True).then_inc(
                                sems[t.engine], 1)
                            undrained.clear()
                        else:
                            ins.then_inc(sems[t.engine], 1)
                    if t.small:
                        undrained.add(t)
            return body

        for ename in ENGINES:
            if not self.per_engine[ename]:
                continue
            body = make_body(ename)
            getattr(block, ename)(body)


def seg512(lo, hi):
    out = []
    a = lo
    while a < hi:
        b = min(hi, (a // 512 + 1) * 512)
        out.append((a, b))
        a = b
    return out


def build_program():
    nc = bass.Bass("TRN2", target_bir_lowering=False,
                   detect_race_conditions=False)

    dx = nc.dram_tensor("x", [S, DM], F32, kind="ExternalInput")
    dxt = nc.dram_tensor("x_t", [DM, S], F32, kind="ExternalInput")
    dwq = nc.dram_tensor("wq_t", [DM, DM], F32, kind="ExternalInput")
    dwk = nc.dram_tensor("wk_t", [DM, DM], F32, kind="ExternalInput")
    dwv = nc.dram_tensor("wv_t", [DM, DM], F32, kind="ExternalInput")
    dwo = nc.dram_tensor("wo_t", [DM, DM], F32, kind="ExternalInput")
    drw = nc.dram_tensor("rw_t", [DM, NE], F32, kind="ExternalInput")
    drb = nc.dram_tensor("rb_rep", [128, NE], F32, kind="ExternalInput")
    dew = nc.dram_tensor("ew_t", [NE, DM, DM], F32, kind="ExternalInput")
    deb = nc.dram_tensor("eb", [NE, DM], F32, kind="ExternalInput")
    didn = nc.dram_tensor("idn_bf", [128, 128], BF16, kind="ExternalInput")
    didnf = nc.dram_tensor("idn_f32", [128, 128], F32, kind="ExternalInput")
    dmask = nc.dram_tensor("maskneg", [128, 128], BF16, kind="ExternalInput")
    dmaskb = nc.dram_tensor("maskbin", [128, 128], BF16, kind="ExternalInput")
    dy = nc.dram_tensor("y", [S, DM], F32, kind="ExternalOutput")

    sch = Sched()

    from contextlib import ExitStack
    ctx = ExitStack()

    def sb(name, shape, dt):
        return ctx.enter_context(nc.sbuf_tensor(name, shape, dt))

    def psum(name, shape):
        return ctx.enter_context(nc.psum_tensor(name, shape, F32))

    X32 = sb("X32", [128, NT * DM], F32)
    XT = sb("XT", [128, NDC * S], BF16)
    arA = sb("arA", [128, 32640], BF16)
    ewr = sb("ewr", [128, 16 * DM], BF16)
    vpB = sb("vpB", [128, 2 * NT * 65], BF16)
    h1a = sb("h1a", [128, DM], F32)
    h1b = sb("h1b", [128, DM], F32)
    idn = sb("idn", [128, 128], BF16)
    idnf = sb("idnf", [128, 128], F32)
    maskneg = sb("maskneg_sb", [128, 128], BF16)
    maskbin = sb("maskbin_sb", [128, 128], BF16)
    rbrep = sb("rb_rep_sb", [128, NE], F32)
    logitA = sb("logitA", [128, NE], F32)
    logitB = sb("logitB", [128, NE], F32)
    maskw = sb("maskw", [128, NT * NE], F32)
    maskwbf = sb("maskwbf", [128, NE], BF16)
    mwtA = sb("mwtA", [NE, 128], BF16)
    mwtB = sb("mwtB", [NE, 128], BF16)
    recipT = sb("recipT", [128, 8], F32)
    recipTb = sb("recipTb", [128, 8], BF16)
    st_s1 = sb("st_s1", [128, 4], F32)
    st_s2 = sb("st_s2", [128, 4], F32)
    st_nm = sb("st_nm", [128, 4], F32)
    st_var = sb("st_var", [128, 4], F32)
    st_ln = sb("st_ln", [128, 4], F32)
    st_rstd = sb("st_rstd", [128, 4], F32)
    st_nmr = sb("st_nmr", [128, 4], F32)
    mscr = sb("mscr", [128, 12], F32)
    lbm = sb("lbm", [128, 16], F32)
    exd = sb("exd", [128, 16], F32)
    dump8 = sb("dump8", [128, 8], F32)
    wbuf = sb("wbuf", [128, NT], F32)

    attnT = arA[:, 0:16384]
    wosb = arA[:, 16384:24576]
    qpq = arA[:, 24576:26624]
    qpk = arA[:, 26624:28672]
    Ptil = [arA[:, 28672:29696], arA[:, 29696:30720]]
    Rbv = arA[:, 30720:31232]
    sumsb = arA[:, 31232:32256]
    ACC = arA[:, 0:16384].bitcast(F32)
    ebsb = arA[0:NE, 28672:29696]
    outf = [arA[:, 24576:26624].bitcast(F32), arA[:, 26624:28672].bitcast(F32)]

    qpqB = X32[:, 14 * DM:15 * DM].bitcast(BF16)
    qpkB = X32[:, 15 * DM:16 * DM].bitcast(BF16)
    sqdump = vpB[:, 0:DM]

    vp = ewr[:, 0:2 * NT * 65]
    wq_sl = [[ewr[:, 2112 + (3 * b + m) * 1024: 2112 + (3 * b + m + 1) * 1024]
              for m in range(3)] for b in range(2)]

    def ewbuf(b):
        return ewr[:, 8192 * b: 8192 * (b + 1)]

    PS = psum("PS", [128, 4096])

    def bank(k, p0=0, p1=128, lo=0, hi=512):
        return PS[p0:p1, 512 * k + lo:512 * k + hi]

    def bankpair(k, p0=0, p1=128, lo=0, hi=1024):
        return PS[p0:p1, 512 * k + lo:512 * k + hi]

    r_X32 = [Res(f"X32_{t}") for t in range(NT)]
    r_XT = Res("XT")
    r_attnT = Res("attnT")
    r_wo = Res("wo")
    r_qpq = Res("qpq")
    r_qpk = Res("qpk")
    r_qpq2 = Res("qpqB")
    r_qpk2 = Res("qpkB")
    r_vp2 = Res("vpB")
    r_Ptil = [Res("PtilA"), Res("PtilB")]
    r_Rb = Res("Rb")
    r_sums = Res("sums")
    r_vp = Res("vp")
    r_wq = [Res("wqkvA"), Res("wqkvB")]
    r_ew = [Res("ewA"), Res("ewB")]
    r_ACC = [Res(f"ACC{t}", overlays=[r_attnT]) for t in range(8)]
    r_out = [Res("outA", overlays=[r_qpq]), Res("outB", overlays=[r_qpk])]
    r_bank = [Res(f"psb{k}") for k in range(8)]
    r_h1 = [Res("h1a"), Res("h1b")]
    r_logit = [Res("logitA"), Res("logitB")]
    r_maskw = [Res(f"maskw{t}") for t in range(NT)]
    r_wbuf = Res("wbuf")
    r_mwt = [Res("mwtA"), Res("mwtB")]
    r_recip = Res("recip")
    r_stats = [Res(f"stats{i}") for i in range(4)]
    r_XTs = [Res(f"XTs{t}", overlays=[r_XT]) for t in range(NT)]
    r_bstg = [Res("bstgA", overlays=[r_Ptil[1]]), Res("bstgB", overlays=[r_sums])]
    r_consts = Res("consts")
    r_ebsb = Res("ebsb", overlays=[r_Ptil[0]])
    r_sqdump = Res("sqdump", overlays=[r_vp2])
    r_ew[0].overlays = [r_vp, r_wq[0], r_wq[1]]
    r_ew[1].overlays = [r_wq[0], r_wq[1]]
    r_X32[14].overlays = [r_qpq2]
    r_X32[15].overlays = [r_qpk2]

    sch.add("sync", lambda e: e.dma_start(idn[:, :], didn[:, :]),
            writes=[r_consts], dma=True, label="dma_idn")
    sch.add("sync", lambda e: e.dma_start(idnf[:, :], didnf[:, :]),
            writes=[r_consts], dma=True, label="dma_idnf")
    sch.add("sync", lambda e: e.dma_start(maskneg[:, :], dmask[:, :]),
            writes=[r_consts], dma=True, label="dma_mask")
    sch.add("sync", lambda e: e.dma_start(maskbin[:, :], dmaskb[:, :]),
            writes=[r_consts], dma=True, label="dma_maskb")
    sch.add("sync", lambda e: e.dma_start(rbrep[:, :], drb[:, :]),
            writes=[r_consts], dma=True, label="dma_rb")
    sch.add("sync",
            lambda e: e.dma_start(
                X32[:, 0:14 * DM].rearrange("p (t n) -> p t n", t=14),
                dx[0:14 * 128, :].rearrange("(t p) n -> p t n", p=128)),
            writes=list(r_X32[:14]), dma=True, label="dma_x32")
    sch.add("gpsimd",
            lambda e: e.dma_start(
                XT[:, :].rearrange("p (c n) -> p c n", c=NDC),
                dxt[:, :].rearrange("(c p) n -> p c n", p=128)),
            writes=[r_XT], dma=True, label="dma_xt")
    sch.add("gpsimd",
            lambda e: e.dma_start(
                wosb[:, :].rearrange("p (c n) -> p c n", c=NDC),
                dwo[:, :].rearrange("(c p) n -> p c n", p=128)),
            writes=[r_wo], dma=True, label="dma_wo")


    def mk_vones(buf, rr, m, t):
        def f(e):
            return e.memset(buf[:, (m * NT + t) * 65 + 64:
                                (m * NT + t) * 65 + 65], 1.0)
        return f
    for m in range(2):
        for t in range(NT):
            sch.add("vector", mk_vones(vp, r_vp, m, t), writes=[r_vp],
                    label="vones")
            sch.add("vector", mk_vones(vpB, r_vp2, m, t), writes=[r_vp2],
                    label="vonesB")

    wmats = [dwq, dwk, dwv]

    def mk_wdma(p, m):
        buf = wq_sl[p % 2][m]

        def f(e):
            src = wmats[m][:, 128 * p:128 * (p + 1)]
            return e.dma_start(
                buf.rearrange("p (c n) -> p c n", c=NDC),
                src.rearrange("(c p) n -> p c n", p=128))
        return f

    def rec_wdma(p):
        for m in range(3):
            sch.add("gpsimd", mk_wdma(p, m), writes=[r_wq[p % 2]],
                    dma=True, label=f"dma_w{p}{m}")

    rec_wdma(0)
    rec_wdma(1)

    rw8 = sb("rw8", [128, NDC * NE], BF16)
    r_rw8 = Res("rw8")
    sch.add("gpsimd",
            lambda e: e.dma_start(
                rw8[:, :].rearrange("p (c n) -> p c n", c=NDC),
                drw[:, :].rearrange("(c p) n -> p c n", p=128)),
            writes=[r_rw8], dma=True, label="dma_rw8")

    filler = []

    qp_sets = [(qpq, qpk, r_qpq, r_qpk), (qpqB, qpkB, r_qpq2, r_qpk2)]
    vp_sets = [(vp, r_vp), (vpB, r_vp2)]

    def queue_proj(p):
        wb = r_wq[p % 2]
        wsl = wq_sl[p % 2]
        qpq_, qpk_, r_qpq_, r_qpk_ = qp_sets[p % 2]
        vp_, r_vp_ = vp_sets[p % 2]
        items = []

        def qk_item(m, seg, dstap, rres, fb):
            def rec():
                def mmf(e):
                    last = None
                    for c in range(NDC):
                        last = e.matmul(
                            bank(fb, lo=0, hi=256),
                            wsl[m][:, 128 * c:128 * (c + 1)],
                            XT[:, 2048 * c + 256 * seg:
                               2048 * c + 256 * (seg + 1)],
                            start=(c == 0), stop=(c == NDC - 1))
                    return last
                sch.add("tensor", mmf, reads=[wb, r_XT], writes=[r_bank[fb]],
                        label=f"fproj{p}_{m}_{seg}")

                def evf(e):
                    return e.tensor_copy(
                        dstap[:, 256 * seg:256 * (seg + 1)],
                        bank(fb, lo=0, hi=256))
                sch.add("vector", evf, reads=[r_bank[fb]], writes=[rres],
                        label=f"fev{p}_{m}_{seg}")
            return rec

        def v_item(g, ti, fb):
            def rec():
                t = 4 * g + ti

                def mmf(e):
                    last = None
                    for c in range(NDC):
                        last = e.matmul(
                            bank(fb, lo=128 * ti, hi=128 * (ti + 1)),
                            XT[:, 2048 * c + 128 * t:
                               2048 * c + 128 * (t + 1)],
                            wsl[2][:, 128 * c:128 * (c + 1)],
                            start=(c == 0), stop=(c == NDC - 1))
                    return last

                def evf(e):
                    last = None
                    for m in range(2):
                        last = e.tensor_copy(
                            vp_[:, (m * NT + t) * 65:
                                (m * NT + t) * 65 + 64],
                            bank(fb, lo=128 * ti + 64 * m,
                                 hi=128 * ti + 64 * (m + 1)))
                    return last
                sch.add("tensor", mmf, reads=[wb, r_XT], writes=[r_bank[fb]],
                        label=f"fprojv{p}_{g}_{ti}")
                sch.add("vector", evf, reads=[r_bank[fb]], writes=[r_vp_],
                        label=f"fevv{p}_{g}_{ti}")
            return rec

        k = 0
        for m, dstap, rres in ((0, qpq_, r_qpq_), (1, qpk_, r_qpk_)):
            for seg in range(8):
                items.append(qk_item(m, seg, dstap, rres, 6 + (k % 2)))
                k += 1
        for g in range(4):
            for ti in range(4):
                items.append(v_item(g, ti, 6 + (k % 2)))
                k += 1
        return items

    def pop_filler(n=1):
        for _ in range(n):
            if filler:
                filler.pop(0)()

    pending_ep = []

    def flush_ep(n=-1):
        while pending_ep and n != 0:
            pending_ep.pop(0)()
            n -= 1

    def attn_head_quarter(p, m, qq, obuf):
        flush_ep(len(pending_ep) - 2)
        h = 2 * p + m
        qh0 = 512 * qq
        jmax = 4 * qq + 3
        hb = 64 * m
        ob = 4 + obuf
        qpq_, qpk_, r_qpq_, r_qpk_ = qp_sets[p % 2]
        vp_, r_vp_ = vp_sets[p % 2]

        def los(j):
            return max(qh0, 128 * j) - qh0

        def mk_S(j, sb_):
            lo = los(j)

            def f(e):
                return e.matmul(
                    bank(sb_, lo=lo, hi=512),
                    qpk_[hb:hb + 64, 128 * j:128 * (j + 1)],
                    qpq_[hb:hb + 64, qh0 + lo:qh0 + 512],
                    start=True, stop=True)
            return f

        def mk_exp(je, pb):
            lo = los(je)

            def f(e):
                return e.activation(Ptil[pb][:, lo:1024],
                                    PS[:, 1024 * pb + lo:1024 * pb + 1024],
                                    AF.Exp, bias=np.float32(-CSOFT),
                                    scale=np.float32(SMSCALE))
            return f

        def mk_PV(j, pb):
            lo = los(j)
            off = 512 * (j % 2)

            def f(e):
                return e.matmul(
                    bank(ob, p1=65, lo=lo, hi=512),
                    vp_[:, (m * NT + j) * 65:(m * NT + j + 1) * 65],
                    Ptil[pb][:, off + lo:off + 512],
                    start=(j == 0), stop=(j == jmax))
            return f

        npairs = (jmax + 1) // 2
        for pi in range(npairs):
            je, jo = 2 * pi, 2 * pi + 1
            pb = pi % 2
            sch.add("tensor", mk_S(je, 2 * pb), reads=[r_qpq_, r_qpk_, r_consts],
                    writes=[r_bank[2 * pb]], label=f"S{h}_{qq}_{je}")
            sch.add("tensor", mk_S(jo, 2 * pb + 1),
                    reads=[r_qpq_, r_qpk_, r_consts],
                    writes=[r_bank[2 * pb + 1]], label=f"S{h}_{qq}_{jo}")
            sch.add("scalar", mk_exp(je, pb),
                    reads=[r_bank[2 * pb], r_bank[2 * pb + 1]],
                    writes=[r_Ptil[pb]], label=f"exp{h}_{qq}_{pi}")
            if jo >= 4 * qq:
                def mk_msk(je=je, jo=jo, pb=pb):
                    def f(e):
                        last = None
                        for j in (je, jo):
                            if j >= 4 * qq:
                                lo = los(j)
                                off = 512 * (j % 2)
                                last = e.tensor_tensor(
                                    Ptil[pb][:, off + lo:off + lo + 128],
                                    Ptil[pb][:, off + lo:off + lo + 128],
                                    maskbin[:, :], OP.mult)
                        return last
                    return f
                sch.add("vector", mk_msk(), reads=[r_Ptil[pb], r_consts],
                        writes=[r_Ptil[pb]], label=f"msk{h}_{qq}_{pi}",
                        small=True)
            if pi >= 1:
                sch.add("tensor", mk_PV(2 * pi - 2, (pi - 1) % 2),
                        reads=[r_Ptil[(pi - 1) % 2], r_vp_],
                        writes=[r_bank[ob]], label=f"PV{h}_{qq}_{2*pi-2}")
                sch.add("tensor", mk_PV(2 * pi - 1, (pi - 1) % 2),
                        reads=[r_Ptil[(pi - 1) % 2], r_vp_],
                        writes=[r_bank[ob]], label=f"PV{h}_{qq}_{2*pi-1}")
            if pi >= 1:
                flush_ep(1)
            pop_filler(1)
        pb = (npairs - 1) % 2
        sch.add("tensor", mk_PV(jmax - 1, pb), reads=[r_Ptil[pb], r_vp_],
                writes=[r_bank[ob]], label=f"PV{h}_{qq}_{jmax-1}")
        sch.add("tensor", mk_PV(jmax, pb), reads=[r_Ptil[pb], r_vp_],
                writes=[r_bank[ob]], label=f"PV{h}_{qq}_{jmax}")
        def ep1():
            def f_sums(e):
                return e.copy(sumsb[0:1, 0:512], bank(ob, p0=64, p1=65))
            sch.add("scalar", f_sums, reads=[r_bank[ob]], writes=[r_sums],
                    label=f"sums{h}_{qq}", small=True)

            def f_T(e):
                last = None
                for i in range(4):
                    last = e.matmul(bank(6, lo=i, hi=i + 1),
                                    sumsb[0:1, 128 * i:128 * (i + 1)],
                                    idn[0:1, 0:1], start=True, stop=True)
                return last
            sch.add("tensor", f_T, reads=[r_sums, r_consts],
                    writes=[r_bank[6]], label=f"epT{h}_{qq}")

            def f_recip(e):
                pass
        def ep2():
            def f_recip(e):
                with nc.allow_low_precision(reason="bf16 softmax recip"):
                    return e.reciprocal(recipTb[:, 0:4], bank(6, hi=4))
            sch.add("vector", f_recip, reads=[r_bank[6]], writes=[r_recip],
                    label=f"recip{h}_{qq}", small=True)

            def f_bc(e):
                last = None
                for i in range(4):
                    last = e.matmul(
                        bank(7, p1=64, lo=128 * i, hi=128 * (i + 1)),
                        recipTb[:, i:i + 1].broadcast_to([128, 64]),
                        idn[:, :], start=True, stop=True)
                return last
            sch.add("tensor", f_bc, reads=[r_recip, r_consts],
                    writes=[r_bank[7]], label=f"epbc{h}_{qq}")

            def f_evR(e):
                return e.tensor_copy(Rbv[0:64, 0:512], bank(7, p1=64))
            sch.add("vector", f_evR, reads=[r_bank[7]], writes=[r_Rb],
                    label=f"evR{h}_{qq}")

            ch = h // 2

            def f_mul(e):
                return e.tensor_tensor(
                    attnT[hb:hb + 64, 2048 * ch + qh0:2048 * ch + qh0 + 512],
                    bank(ob, p1=64), Rbv[0:64, 0:512], OP.mult)
            sch.add("vector", f_mul, reads=[r_bank[ob], r_Rb],
                    writes=[r_attnT], label=f"emul{h}_{qq}")
        pending_ep.append(ep1)
        pending_ep.append(ep2)

    obuf = 0
    for p in range(NH // 2):
        if p == 0:
            for it in queue_proj(0):
                it()
        if p + 1 < NH // 2:
            filler.extend(queue_proj(p + 1))
        if p + 2 < NH // 2:
            rec_wdma(p + 2)
        for m in range(2):
            for qq in range(4):
                attn_head_quarter(p, m, qq, obuf)
                obuf ^= 1
        while filler:
            pop_filler(1)
    flush_ep()

    for t in (14, 15):
        def f_xlate(t=t):
            def f(e):
                return e.dma_start(X32[:, DM * t:DM * (t + 1)],
                                   dx[128 * t:128 * (t + 1), :])
            return f
        sch.add("sync", f_xlate(), writes=[r_X32[t]], dma=True,
                label=f"dma_x32_{t}")

    pool = [(bankpair(0), [r_bank[0], r_bank[1]]),
            (bankpair(2), [r_bank[2], r_bank[3]]),
            (bankpair(4), [r_bank[4], r_bank[5]])]
    pooli = [0]

    def take_pool():
        ps, rr = pool[pooli[0] % 3]
        pooli[0] += 1
        return ps, rr

    LNEPS = 1e-5

    def ln_tasks(t, src_ps, rsrc_ps, sset, phase):
        hb = h1a if (t % 2 == 0) else h1b
        rhb = r_h1[t % 2]
        sl = slice(2 * 0 + sset, 2 * 0 + sset + 1)
        is2 = (phase == 2)
        accslot = t % 8

        def f_add(e):
            if is2:
                return e.scalar_tensor_tensor(
                    hb[:, :], ACC[:, DM * accslot:DM * (accslot + 1)],
                    wbuf[:, t:t + 1],
                    X32[:, DM * t:DM * (t + 1)], OP.mult, OP.add,
                    accum_out=st_s1[:, sset:sset + 1])
            return e.scalar_tensor_tensor(
                hb[:, :], src_ps[:, 0:1024], 1.0,
                X32[:, DM * t:DM * (t + 1)], OP.mult, OP.add,
                accum_out=st_s1[:, sset:sset + 1])
        reads = ([r_ACC[accslot]] if is2 else list(rsrc_ps)) + [r_X32[t]]
        if is2:
            reads.append(r_wbuf)
        sch.add("vector", f_add, reads=reads, writes=[rhb, r_stats[sset]],
                label=f"lnadd{phase}_{t}", small=True)

        def f_sq(e):
            return e.scalar_tensor_tensor(
                sqdump[:, :], hb[:, :], 1.0, hb[:, :], OP.mult, OP.mult,
                accum_out=st_s2[:, sset:sset + 1])
        sch.add("vector", f_sq, reads=[rhb],
                writes=[r_stats[sset], r_sqdump],
                label=f"lnsq{phase}_{t}", small=True)

        def f_nm(e):
            return e.activation(st_nm[:, sset:sset + 1],
                                st_s1[:, sset:sset + 1], AF.Copy,
                                scale=np.float32(-1.0 / DM))
        sch.add("scalar", f_nm, reads=[r_stats[sset]], writes=[r_stats[sset]],
                label=f"lnnm{phase}_{t}", small=True)

        def f_e2(e):
            return e.activation(st_var[:, sset:sset + 1],
                                st_s2[:, sset:sset + 1], AF.Copy,
                                scale=np.float32(1.0 / DM))
        sch.add("scalar", f_e2, reads=[r_stats[sset]], writes=[r_stats[sset]],
                label=f"lne2{phase}_{t}", small=True)

        def f_m2(e):
            return e.scalar_tensor_tensor(
                st_var[:, sset:sset + 1], st_nm[:, sset:sset + 1],
                st_nm[:, sset:sset + 1], st_var[:, sset:sset + 1],
                OP.mult, OP.subtract)
        sch.add("vector", f_m2, reads=[r_stats[sset]],
                writes=[r_stats[sset]], label=f"lnm2{phase}_{t}", small=True)

        def f_ln(e):
            return e.activation(st_ln[:, sset:sset + 1],
                                st_var[:, sset:sset + 1], AF.Ln,
                                bias=np.float32(LNEPS),
                                scale=np.float32(-1.0))
        sch.add("scalar", f_ln, reads=[r_stats[sset]], writes=[r_stats[sset]],
                label=f"lnln{phase}_{t}", small=True)

        def f_rstd(e):
            return e.activation(st_rstd[:, sset:sset + 1],
                                st_ln[:, sset:sset + 1], AF.Exp,
                                scale=np.float32(-0.5))
        sch.add("scalar", f_rstd, reads=[r_stats[sset]], writes=[r_stats[sset]],
                label=f"lnrstd{phase}_{t}", small=True)

        def f_nmr(e):
            return e.tensor_scalar(st_nmr[:, sset:sset + 1],
                                   st_nm[:, sset:sset + 1],
                                   st_rstd[:, sset:sset + 1], None, OP.mult)
        sch.add("vector", f_nmr, reads=[r_stats[sset]], writes=[r_stats[sset]],
                label=f"lnnmr{phase}_{t}", small=True)

        if not is2:
            def f_norm(e):
                return e.activation(X32[:, DM * t:DM * (t + 1)], hb[:, :],
                                    AF.Identity, bias=st_nmr[:, sset:sset + 1],
                                    scale=st_rstd[:, sset:sset + 1])
            sch.add("scalar", f_norm, reads=[rhb, r_stats[sset]],
                    writes=[r_X32[t]], label=f"lnnorm{phase}_{t}")
        else:
            ob = t % 2

            def f_norm(e):
                return e.activation(outf[ob][:, :], hb[:, :],
                                    AF.Identity, bias=st_nmr[:, sset:sset + 1],
                                    scale=st_rstd[:, sset:sset + 1])
            sch.add("scalar", f_norm, reads=[rhb, r_stats[sset]],
                    writes=[r_out[ob]], label=f"lnnorm{phase}_{t}")

            def f_st(e):
                return e.dma_start(dy[128 * t:128 * (t + 1), :],
                                   outf[ob][:, :])
            sch.add("sync", f_st, reads=[r_out[ob]], dma=True,
                    label=f"store{t}")

    def rec_oln(t):
        ps, rps = take_pool()

        def f_oproj(t=t, ps=ps):
            def f(e):
                last = None
                for c in range(NDC):
                    for sg in range(2):
                        last = e.matmul(
                            ps[:, 512 * sg:512 * (sg + 1)],
                            attnT[:, 2048 * c + 128 * t:2048 * c + 128 * (t + 1)],
                            wosb[:, 1024 * c + 512 * sg:1024 * c + 512 * (sg + 1)],
                            start=(c == 0), stop=(c == NDC - 1))
                return last
            return f
        sch.add("tensor", f_oproj(), reads=[r_attnT, r_wo], writes=list(rps),
                label=f"oproj{t}")
        ln_tasks(t, ps, rps, t % 2, phase=1)

    def rec_dphase(t):
        ps, rps = take_pool()

        def f_xt(t=t, ps=ps):
            def f(e):
                last = None
                for c in range(NDC):
                    last = e.matmul(
                        ps[:, 128 * c:128 * (c + 1)],
                        X32[:, DM * t + 128 * c:DM * t + 128 * (c + 1)],
                        idnf[:, :], start=True, stop=True)
                return last
            return f
        sch.add("tensor", f_xt(), reads=[r_X32[t], r_consts], writes=list(rps),
                label=f"xmt{t}")

        def f_evxt(t=t, ps=ps):
            def f(e):
                return e.activation(
                    XT[:, :].rearrange("p (c n) -> p c n", c=NDC)[:, :, 128 * t:128 * (t + 1)],
                    ps[:, 0:1024].rearrange("p (c n) -> p c n", c=NDC),
                    AF.Copy)
            return f
        sch.add("scalar", f_evxt(), reads=list(rps), writes=[r_XTs[t]],
                label=f"evxmt{t}")

        lb = logitA if t % 2 == 0 else logitB
        rlb = r_logit[t % 2]

        rbk = 6 + (t % 2)

        def f_rmm(t=t, rbk=rbk):
            def f(e):
                last = None
                for c in range(NDC):
                    last = e.matmul(
                        bank(rbk, hi=NE),
                        XT[:, 2048 * c + 128 * t:2048 * c + 128 * (t + 1)],
                        rw8[:, NE * c:NE * (c + 1)],
                        start=(c == 0), stop=(c == NDC - 1))
                return last
            return f
        sch.add("tensor", f_rmm(), reads=[r_XTs[t], r_rw8],
                writes=[r_bank[rbk]], label=f"rmm{t}")

        def f_rev(lb=lb, rbk=rbk):
            def f(e):
                return e.tensor_add(lb[:, :], bank(rbk, hi=NE), rbrep[:, :])
            return f
        sch.add("vector", f_rev(), reads=[r_bank[rbk], r_consts], writes=[rlb],
                label=f"rev{t}", small=True)

        s_ = t % 2
        sr = 2 + s_

        def f_rtree(lb=lb, s_=s_, sr=sr):
            def f(e):
                e.tensor_max(mscr[:, 6 * s_:6 * s_ + 4], lb[:, 0:4], lb[:, 4:8])
                e.drain(fusable=True)
                e.tensor_max(mscr[:, 6 * s_ + 4:6 * s_ + 6],
                             mscr[:, 6 * s_:6 * s_ + 2],
                             mscr[:, 6 * s_ + 2:6 * s_ + 4])
                e.drain(fusable=True)
                return e.tensor_max(st_s2[:, sr:sr + 1],
                                    mscr[:, 6 * s_ + 4:6 * s_ + 5],
                                    mscr[:, 6 * s_ + 5:6 * s_ + 6])
            return f
        sch.add("vector", f_rtree(), reads=[rlb], writes=[r_stats[sr]],
                label=f"rtree{t}", small=True)

        def f_rsub(lb=lb, s_=s_, sr=sr):
            def f(e):
                return e.tensor_scalar(lbm[:, 8 * s_:8 * s_ + 8], lb[:, :],
                                       st_s2[:, sr:sr + 1], None, OP.subtract)
            return f
        sch.add("vector", f_rsub(), reads=[rlb, r_stats[sr]],
                writes=[r_stats[sr]], label=f"rsub{t}", small=True)

        def f_rexp(s_=s_):
            def f(e):
                return e.activation(exd[:, 8 * s_:8 * s_ + 8],
                                    lbm[:, 8 * s_:8 * s_ + 8], AF.Exp)
            return f
        sch.add("scalar", f_rexp(), reads=[r_stats[sr]], writes=[r_stats[sr]],
                label=f"rexp{t}", small=True)

        def f_rsum(s_=s_, sr=sr):
            def f(e):
                return e.scalar_tensor_tensor(
                    dump8[:, :], exd[:, 8 * s_:8 * s_ + 8], 1.0,
                    exd[:, 8 * s_:8 * s_ + 8], OP.mult, OP.bypass,
                    accum_out=st_var[:, sr:sr + 1])
            return f
        sch.add("vector", f_rsum(), reads=[r_stats[sr]], writes=[r_stats[sr]],
                label=f"rsum{t}", small=True)

        def f_rmask(s_=s_, t=t):
            def f(e):
                return e.tensor_single_scalar(maskw[:, NE * t:NE * (t + 1)],
                                              lbm[:, 8 * s_:8 * s_ + 8],
                                              0.0, OP.is_ge)
            return f
        sch.add("vector", f_rmask(), reads=[r_stats[sr]],
                writes=[r_maskw[t]], label=f"rmask{t}", small=True)

        def f_wrecip(sr=sr, t=t):
            def f(e):
                return e.reciprocal(wbuf[:, t:t + 1], st_var[:, sr:sr + 1])
            return f
        sch.add("vector", f_wrecip(), reads=[r_stats[sr]], writes=[r_wbuf],
                label=f"wrecip{t}", small=True)

    def rec_ewdma(li, ei):
        b = li % 2

        def f(e):
            return e.dma_start(
                ewbuf(b).rearrange("p (c n) -> p c n", c=NDC),
                dew[ei].rearrange("(c p) n -> p c n", p=128))
        sch.add("gpsimd", f, writes=[r_ew[b]], dma=True,
                label=f"dma_ew{li}")

    def rec_mcast(t):
        rmt = r_mwt[t % 2]

        def f_mcast(e):
            return e.activation(maskwbf[:, :], maskw[:, NE * t:NE * (t + 1)],
                                AF.Copy)
        sch.add("scalar", f_mcast, reads=[r_maskw[t]], writes=[rmt],
                label=f"mcast{t}", small=True)

    def rec_tile_init(t, li):
        rec_mcast(t)
        mt = mwtA if t % 2 == 0 else mwtB
        rmt = r_mwt[t % 2]
        accslot = t % 8
        mbk = 6 + (t % 2)
        eb_ = li % 2

        ps_e, rps_e = take_pool()

        def f_emm(t=t, ps=ps_e, eb_=eb_):
            def f(e):
                last = None
                buf = ewbuf(eb_)
                for c in range(NDC):
                    for sg in range(2):
                        last = e.matmul(
                            ps[:, 512 * sg:512 * (sg + 1)],
                            XT[:, 2048 * c + 128 * t:
                               2048 * c + 128 * (t + 1)],
                            buf[:, 1024 * c + 512 * sg:
                                1024 * c + 512 * (sg + 1)],
                            start=(c == 0), stop=(c == NDC - 1))
                return last
            return f
        sch.add("tensor", f_emm(), reads=[r_XTs[t], r_ew[eb_]],
                writes=list(rps_e), label=f"emm{li}_{t}")

        def f_mT(e, mbk=mbk):
            return e.matmul(bank(mbk, p1=NE, lo=256, hi=384), maskwbf[:, :],
                            idn[:, :], start=True, stop=True)
        sch.add("tensor", f_mT, reads=[rmt, r_consts], writes=[r_bank[mbk]],
                label=f"mT{t}")

        def f_mev(e, mbk=mbk):
            return e.activation(mt[:, :], bank(mbk, p1=NE, lo=256, hi=384),
                                AF.Copy)
        sch.add("scalar", f_mev, reads=[r_bank[mbk]], writes=[rmt],
                label=f"mev{t}", small=True)

        ps_b, rps_b = take_pool()

        def f_bias(ps=ps_b):
            def f(e):
                last = None
                for sg in range(2):
                    last = e.matmul(ps[:, 512 * sg:512 * (sg + 1)],
                                    mt[:, :],
                                    ebsb[:, 512 * sg:512 * (sg + 1)],
                                    start=True, stop=True)
                return last
            return f
        sch.add("tensor", f_bias(), reads=[rmt, r_ebsb], writes=list(rps_b),
                label=f"bias{t}")

        bstg = arA[:, 29696:30720] if t % 2 == 0 else arA[:, 31232:32256]
        rbs = r_bstg[t % 2]

        def f_bst(e, ps=ps_b, bstg=bstg):
            return e.activation(bstg, ps[:, 0:1024], AF.Copy)
        sch.add("scalar", f_bst, reads=list(rps_b), writes=[rbs],
                label=f"bstg{t}")

        def f_estt(t=t, pe=ps_e, bstg=bstg, accslot=accslot):
            def f(e):
                return e.scalar_tensor_tensor(
                    ACC[:, DM * accslot:DM * (accslot + 1)],
                    pe[:, 0:1024],
                    maskw[:, NE * t:NE * t + 1],
                    bstg,
                    OP.mult, OP.add)
            return f
        sch.add("vector", f_estt(),
                reads=list(rps_e) + [rbs, r_maskw[t]],
                writes=[r_ACC[accslot]], label=f"estt{li}_{t}")

    def rec_expert_tile(li, e_i, t, init=False):
        eb_ = li % 2
        ps, rps = take_pool()
        accslot = t % 8

        def f_emm(t=t, ps=ps, eb_=eb_):
            def f(e):
                last = None
                buf = ewbuf(eb_)
                for c in range(NDC):
                    for sg in range(2):
                        last = e.matmul(
                            ps[:, 512 * sg:512 * (sg + 1)],
                            XT[:, 2048 * c + 128 * t:
                               2048 * c + 128 * (t + 1)],
                            buf[:, 1024 * c + 512 * sg:
                                1024 * c + 512 * (sg + 1)],
                            start=(c == 0), stop=(c == NDC - 1))
                return last
            return f
        sch.add("tensor", f_emm(),
                reads=[r_XTs[t], r_ew[eb_]],
                writes=list(rps), label=f"emm{li}_{t}")

        def f_estt(t=t, ps=ps, e_i=e_i, accslot=accslot, init=init):
            def f(e):
                return e.scalar_tensor_tensor(
                    ACC[:, DM * accslot:DM * (accslot + 1)],
                    ps[:, 0:1024],
                    maskw[:, NE * t + e_i:NE * t + e_i + 1],
                    ACC[:, DM * accslot:DM * (accslot + 1)],
                    OP.mult, OP.bypass if init else OP.add)
            return f
        rdd = list(rps) + [r_maskw[t]]
        if not init:
            rdd.append(r_ACC[accslot])
        sch.add("vector", f_estt(), reads=rdd,
                writes=[r_ACC[accslot]], label=f"estt{li}_{t}")

    for t in range(NT):
        rec_oln(t)
    sch.add("gpsimd", lambda e: e.dma_start(ebsb, deb[:, :]),
            writes=[r_ebsb], dma=True, label="dma_eb")
    rec_ewdma(0, 0)
    for t in range(8):
        rec_dphase(t)
        rec_tile_init(t, 0)
    rec_ewdma(1, 1)
    for e_i in range(1, NE):
        li = e_i
        rec_dphase(7 + e_i)
        for t in range(8):
            rec_expert_tile(li, e_i, t)
            if e_i == NE - 1 and t == 3:
                rec_dphase(15)
        rec_ewdma(li + 1, (li + 1) % NE)
    for t in range(8, 16):
        ln_tasks(t - 8, None, None, (t - 8) % 2, phase=2)
        rec_tile_init(t, 8)
    rec_ewdma(9, 1)
    for e_i in range(1, NE):
        li = 8 + e_i
        for t in range(8, 16):
            rec_expert_tile(li, e_i, t)
            if e_i == NE - 1:
                ln_tasks(t, None, None, t % 2, phase=2)
        if li + 1 < 16:
            rec_ewdma(li + 1, (li + 1) % NE)

    store_tasks = [t for t in sch.per_engine["sync"] if t.label.startswith("store")]

    def f_fin(e):
        return None
    fin = Task("sync", f_fin, False, "final")
    fin.rec_idx = len(sch.tasks)
    fin.deps = store_tasks
    sch.tasks.append(fin)
    sch.per_engine["sync"].append(fin)

    with ctx:
        sem_t = ctx.enter_context(nc.semaphore("sem_t"))
        sem_v = ctx.enter_context(nc.semaphore("sem_v"))
        sem_s = ctx.enter_context(nc.semaphore("sem_s"))
        sem_g = ctx.enter_context(nc.semaphore("sem_g"))
        sem_sp = ctx.enter_context(nc.semaphore("sem_sp"))
        dma_sems = [ctx.enter_context(nc.semaphore(f"dq{i}"))
                    for i in range(len(sch.dma_tasks))]
        with nc.Block() as block:
            sems = {"tensor": sem_t, "vector": sem_v, "scalar": sem_s,
                    "sync": sem_sp, "gpsimd": sem_g}
            sch.finalize(nc, block, sems, dma_sems)
    return nc


_NC_CACHE = None


def kernel(L0_attn_x, L0_attn_W_Q, L0_attn_W_K, L0_attn_W_V, L0_attn_W_O,
           L0_ffn_router_w, L0_ffn_router_b, L0_ffn_expert_w, L0_ffn_expert_b):
    global _NC_CACHE
    x = np.ascontiguousarray(np.asarray(L0_attn_x, dtype=np.float32))
    wq_t = np.ascontiguousarray(np.asarray(L0_attn_W_Q, np.float32).T)
    wk_t = np.ascontiguousarray(np.asarray(L0_attn_W_K, np.float32).T)
    wv_t = np.ascontiguousarray(np.asarray(L0_attn_W_V, np.float32).T)
    wo_t = np.ascontiguousarray(np.asarray(L0_attn_W_O, np.float32).T)
    rw_t = np.ascontiguousarray(np.asarray(L0_ffn_router_w, np.float32).T)
    rb_rep = np.ascontiguousarray(
        np.broadcast_to(np.asarray(L0_ffn_router_b, np.float32), (128, NE)))
    ew_t = np.ascontiguousarray(
        np.asarray(L0_ffn_expert_w, np.float32).transpose(0, 2, 1))
    eb = np.ascontiguousarray(np.asarray(L0_ffn_expert_b, np.float32))

    idn_bf = np.eye(128, dtype=BFNP)
    idn_f32 = np.eye(128, dtype=np.float32)
    kk, qq = np.meshgrid(np.arange(128), np.arange(128), indexing="ij")
    maskneg = np.where(qq >= kk, 0.0, -1e9).astype(BFNP)
    maskbin = np.where(qq >= kk, 1.0, 0.0).astype(BFNP)

    if _NC_CACHE is None:
        _NC_CACHE = build_program()
    nc = _NC_CACHE

    in_maps = []
    for b in range(8):
        xb = np.ascontiguousarray(x[b])
        in_maps.append({
            "x": xb,
            "x_t": np.ascontiguousarray(xb.T),
            "wq_t": wq_t, "wk_t": wk_t, "wv_t": wv_t, "wo_t": wo_t,
            "rw_t": rw_t, "rb_rep": rb_rep, "ew_t": ew_t, "eb": eb,
            "idn_bf": idn_bf, "idn_f32": idn_f32, "maskneg": maskneg,
            "maskbin": maskbin,
        })
    res = run_bass_kernel_spmd(nc, in_maps, core_ids=list(range(8)))
    out = np.stack([np.asarray(res.results[b]["y"], dtype=np.float32)
                    for b in range(8)], axis=0)
    return out


# version 33 (final)
# speedup vs baseline: 1.0522x; 1.0468x over previous
import numpy as np
import ml_dtypes

import concourse.bass as bass
import concourse.mybir as mybir
from concourse.bass_utils import run_bass_kernel_spmd

F32 = mybir.dt.float32
BF16 = mybir.dt.bfloat16
AF = mybir.ActivationFunctionType
OP = mybir.AluOpType
BFNP = ml_dtypes.bfloat16

S = 2048
DM = 1024
NH = 16
DH = 64
NE = 8
NT = 16
NDC = 8
CSOFT = 15.0
SMSCALE = 0.125

ENGINES = ("tensor", "vector", "scalar", "sync", "gpsimd")


class Res:
    __slots__ = ("name", "writer", "readers", "overlays", "absorbed")

    def __init__(self, name, overlays=()):
        self.name = name
        self.writer = None
        self.readers = []
        self.overlays = list(overlays)
        self.absorbed = False

    def _absorb(self):
        if self.absorbed or not self.overlays:
            return []
        self.absorbed = True
        deps = []
        for o in self.overlays:
            if o.writer is not None:
                deps.append(o.writer)
            deps.extend(o.readers)
        return deps


class Task:
    __slots__ = ("engine", "emit", "deps", "need_inc", "seq", "dma", "qidx",
                 "rec_idx", "label", "sem_idx", "small")

    def __init__(self, engine, emit, dma, label):
        self.engine = engine
        self.emit = emit
        self.deps = []
        self.need_inc = False
        self.seq = None
        self.dma = dma
        self.qidx = None
        self.rec_idx = None
        self.label = label
        self.small = False


class Sched:
    def __init__(self):
        self.tasks = []
        self.per_engine = {e: [] for e in ENGINES}
        self.dma_count = {"sync": 0, "gpsimd": 0}
        self.dma_tasks = []

    def add(self, engine, emit, reads=(), writes=(), deps=(), dma=False,
            label="", small=False):
        t = Task(engine, emit, dma, label)
        t.small = small
        t.rec_idx = len(self.tasks)
        dd = set()
        for d in deps:
            if d is not None:
                dd.add(d)
        for r in reads:
            for d in r._absorb():
                dd.add(d)
            if r.writer is not None:
                dd.add(r.writer)
            r.readers.append(t)
        for w in writes:
            for d in w._absorb():
                dd.add(d)
            if w.writer is not None:
                dd.add(w.writer)
            for rd in w.readers:
                dd.add(rd)
            w.writer = t
            w.readers = []
        for d in dd:
            if d is t:
                continue
            assert d.rec_idx < t.rec_idx, (d.label, t.label)
            t.deps.append(d)
        self.tasks.append(t)
        self.per_engine[engine].append(t)
        if dma:
            t.qidx = self.dma_count[engine]
            self.dma_count[engine] += 1
            t.sem_idx = len(self.dma_tasks)
            self.dma_tasks.append(t)
        return t

    def finalize(self, nc, block, sems, dma_sems):
        for t in self.tasks:
            if t.dma:
                t.need_inc = True
            for d in t.deps:
                if d.dma or d.engine != t.engine:
                    d.need_inc = True
        for e in ENGINES:
            seq = 0
            for t in self.per_engine[e]:
                if t.dma:
                    continue
                if t.need_inc:
                    seq += 1
                    t.seq = seq

        sched = self
        trace = {}
        Sched.last_trace = trace

        def make_body(ename):
            def body(eng):
                water = {}
                tr = trace.setdefault(ename, [])
                undrained = set()
                for t in sched.per_engine[ename]:
                    waits = []
                    for d in t.deps:
                        if d.dma:
                            key = ("d", d.sem_idx)
                            val = 16
                            sem = dma_sems[d.sem_idx]
                        elif d.engine == t.engine:
                            continue
                        else:
                            key = ("e", d.engine)
                            val = d.seq
                            sem = sems[d.engine]
                        if water.get(key, 0) >= val:
                            continue
                        water[key] = val
                        waits.append((key, val))
                        eng.wait_ge(sem, val)
                    if ename in ("vector", "scalar"):
                        if any((d.engine == ename and not d.dma
                                and d in undrained) for d in t.deps):
                            eng.drain(fusable=True)
                            undrained.clear()
                    tr.append((t.label, t.seq, waits,
                               [(d.engine, d.label, d.seq if not d.dma else d.qidx)
                                for d in t.deps]))
                    ins = t.emit(eng)
                    if t.dma:
                        ins.then_inc(dma_sems[t.sem_idx], 16)
                    elif t.need_inc:
                        if ename in ("vector", "scalar") and t.small:
                            eng.drain(fusable=True).then_inc(
                                sems[t.engine], 1)
                            undrained.clear()
                        else:
                            ins.then_inc(sems[t.engine], 1)
                    if t.small:
                        undrained.add(t)
            return body

        for ename in ENGINES:
            if not self.per_engine[ename]:
                continue
            body = make_body(ename)
            getattr(block, ename)(body)


def seg512(lo, hi):
    out = []
    a = lo
    while a < hi:
        b = min(hi, (a // 512 + 1) * 512)
        out.append((a, b))
        a = b
    return out


def build_program():
    nc = bass.Bass("TRN2", target_bir_lowering=False,
                   detect_race_conditions=False)

    dx = nc.dram_tensor("x", [S, DM], F32, kind="ExternalInput")
    dxt = nc.dram_tensor("x_t", [DM, S], F32, kind="ExternalInput")
    dwq = nc.dram_tensor("wq_t", [DM, DM], F32, kind="ExternalInput")
    dwk = nc.dram_tensor("wk_t", [DM, DM], F32, kind="ExternalInput")
    dwv = nc.dram_tensor("wv_t", [DM, DM], F32, kind="ExternalInput")
    dwo = nc.dram_tensor("wo_t", [DM, DM], F32, kind="ExternalInput")
    drw = nc.dram_tensor("rw_t", [DM, NE], F32, kind="ExternalInput")
    drb = nc.dram_tensor("rb_rep", [128, NE], F32, kind="ExternalInput")
    dew = nc.dram_tensor("ew_t", [NE, DM, DM], F32, kind="ExternalInput")
    deb = nc.dram_tensor("eb", [NE, DM], F32, kind="ExternalInput")
    didn = nc.dram_tensor("idn_bf", [128, 128], BF16, kind="ExternalInput")
    didnf = nc.dram_tensor("idn_f32", [128, 128], F32, kind="ExternalInput")
    dmask = nc.dram_tensor("maskneg", [128, 128], BF16, kind="ExternalInput")
    dmaskb = nc.dram_tensor("maskbin", [128, 128], BF16, kind="ExternalInput")
    dy = nc.dram_tensor("y", [S, DM], F32, kind="ExternalOutput")

    sch = Sched()

    from contextlib import ExitStack
    ctx = ExitStack()

    def sb(name, shape, dt):
        return ctx.enter_context(nc.sbuf_tensor(name, shape, dt))

    def psum(name, shape):
        return ctx.enter_context(nc.psum_tensor(name, shape, F32))

    X32 = sb("X32", [128, NT * DM], F32)
    XT = sb("XT", [128, NDC * S], BF16)
    arA = sb("arA", [128, 32640], BF16)
    ewr = sb("ewr", [128, 16 * DM], BF16)
    vpB = sb("vpB", [128, 2 * NT * 65], BF16)
    h1a = sb("h1a", [128, DM], F32)
    h1b = sb("h1b", [128, DM], F32)
    idn = sb("idn", [128, 128], BF16)
    idnf = sb("idnf", [128, 128], F32)
    maskneg = sb("maskneg_sb", [128, 128], BF16)
    maskbin = sb("maskbin_sb", [128, 128], BF16)
    rbrep = sb("rb_rep_sb", [128, NE], F32)
    logitA = sb("logitA", [128, NE], F32)
    logitB = sb("logitB", [128, NE], F32)
    maskw = sb("maskw", [128, NT * NE], F32)
    maskwbf = sb("maskwbf", [128, NE], BF16)
    mwtA = sb("mwtA", [NE, 128], BF16)
    mwtB = sb("mwtB", [NE, 128], BF16)
    recipT = sb("recipT", [128, 8], F32)
    recipTb = sb("recipTb", [128, 8], BF16)
    st_s1 = sb("st_s1", [128, 4], F32)
    st_s2 = sb("st_s2", [128, 4], F32)
    st_nm = sb("st_nm", [128, 4], F32)
    st_var = sb("st_var", [128, 4], F32)
    st_ln = sb("st_ln", [128, 4], F32)
    st_rstd = sb("st_rstd", [128, 4], F32)
    st_nmr = sb("st_nmr", [128, 4], F32)
    mscr = sb("mscr", [128, 12], F32)
    lbm = sb("lbm", [128, 16], F32)
    exd = sb("exd", [128, 16], F32)
    dump8 = sb("dump8", [128, 8], F32)
    wbuf = sb("wbuf", [128, NT], F32)

    attnT = arA[:, 0:16384]
    wosb = arA[:, 16384:24576]
    qpq = arA[:, 24576:26624]
    qpk = arA[:, 26624:28672]
    Ptil = [arA[:, 28672:29696], arA[:, 29696:30720]]
    Rbv = arA[:, 30720:31232]
    sumsb = arA[:, 31232:32256]
    ACC = arA[:, 0:16384].bitcast(F32)
    ebsb = arA[0:NE, 28672:29696]
    outf = [arA[:, 24576:26624].bitcast(F32), arA[:, 26624:28672].bitcast(F32)]

    qpqB = X32[:, 14 * DM:15 * DM].bitcast(BF16)
    qpkB = X32[:, 15 * DM:16 * DM].bitcast(BF16)
    sqdump = vpB[:, 0:DM]

    vp = ewr[:, 0:2 * NT * 65]
    wq_sl = [[ewr[:, 2112 + (3 * b + m) * 1024: 2112 + (3 * b + m + 1) * 1024]
              for m in range(3)] for b in range(2)]

    def ewbuf(b):
        return ewr[:, 8192 * b: 8192 * (b + 1)]

    PS = psum("PS", [128, 4096])

    def bank(k, p0=0, p1=128, lo=0, hi=512):
        return PS[p0:p1, 512 * k + lo:512 * k + hi]

    def bankpair(k, p0=0, p1=128, lo=0, hi=1024):
        return PS[p0:p1, 512 * k + lo:512 * k + hi]

    r_X32 = [Res(f"X32_{t}") for t in range(NT)]
    r_XT = Res("XT")
    r_attnT = Res("attnT")
    r_wo = Res("wo")
    r_qpq = Res("qpq")
    r_qpk = Res("qpk")
    r_qpq2 = Res("qpqB")
    r_qpk2 = Res("qpkB")
    r_vp2 = Res("vpB")
    r_Ptil = [Res("PtilA"), Res("PtilB")]
    r_Rb = Res("Rb")
    r_sums = Res("sums")
    r_vp = Res("vp")
    r_wq = [Res("wqkvA"), Res("wqkvB")]
    r_ew = [Res("ewA"), Res("ewB")]
    r_ACC = [Res(f"ACC{t}", overlays=[r_attnT]) for t in range(8)]
    r_out = [Res("outA", overlays=[r_qpq]), Res("outB", overlays=[r_qpk])]
    r_bank = [Res(f"psb{k}") for k in range(8)]
    r_h1 = [Res("h1a"), Res("h1b")]
    r_logit = [Res("logitA"), Res("logitB")]
    r_maskw = [Res(f"maskw{t}") for t in range(NT)]
    r_wbuf = Res("wbuf")
    r_mwt = [Res("mwtA"), Res("mwtB")]
    r_recip = Res("recip")
    r_stats = [Res(f"stats{i}") for i in range(4)]
    r_XTs = [Res(f"XTs{t}", overlays=[r_XT]) for t in range(NT)]
    r_bstg = [Res("bstgA", overlays=[r_Ptil[1]]), Res("bstgB", overlays=[r_sums])]
    r_consts = Res("consts")
    r_ebsb = Res("ebsb", overlays=[r_Ptil[0]])
    r_sqdump = Res("sqdump", overlays=[r_vp2])
    r_ew[0].overlays = [r_vp, r_wq[0], r_wq[1]]
    r_ew[1].overlays = [r_wq[0], r_wq[1]]
    r_X32[14].overlays = [r_qpq2]
    r_X32[15].overlays = [r_qpk2]

    sch.add("sync", lambda e: e.dma_start(idn[:, :], didn[:, :]),
            writes=[r_consts], dma=True, label="dma_idn")
    sch.add("sync", lambda e: e.dma_start(idnf[:, :], didnf[:, :]),
            writes=[r_consts], dma=True, label="dma_idnf")
    sch.add("sync", lambda e: e.dma_start(maskneg[:, :], dmask[:, :]),
            writes=[r_consts], dma=True, label="dma_mask")
    sch.add("sync", lambda e: e.dma_start(maskbin[:, :], dmaskb[:, :]),
            writes=[r_consts], dma=True, label="dma_maskb")
    sch.add("sync", lambda e: e.dma_start(rbrep[:, :], drb[:, :]),
            writes=[r_consts], dma=True, label="dma_rb")
    sch.add("sync",
            lambda e: e.dma_start(
                X32[:, 0:14 * DM].rearrange("p (t n) -> p t n", t=14),
                dx[0:14 * 128, :].rearrange("(t p) n -> p t n", p=128)),
            writes=list(r_X32[:14]), dma=True, label="dma_x32")
    sch.add("gpsimd",
            lambda e: e.dma_start(
                XT[:, :].rearrange("p (c n) -> p c n", c=NDC),
                dxt[:, :].rearrange("(c p) n -> p c n", p=128)),
            writes=[r_XT], dma=True, label="dma_xt")
    sch.add("gpsimd",
            lambda e: e.dma_start(
                wosb[:, :].rearrange("p (c n) -> p c n", c=NDC),
                dwo[:, :].rearrange("(c p) n -> p c n", p=128)),
            writes=[r_wo], dma=True, label="dma_wo")


    def mk_vones(buf, rr, m, t):
        def f(e):
            return e.memset(buf[:, (m * NT + t) * 65 + 64:
                                (m * NT + t) * 65 + 65], 1.0)
        return f
    for m in range(2):
        for t in range(NT):
            sch.add("vector", mk_vones(vp, r_vp, m, t), writes=[r_vp],
                    label="vones")
            sch.add("vector", mk_vones(vpB, r_vp2, m, t), writes=[r_vp2],
                    label="vonesB")

    wmats = [dwq, dwk, dwv]

    def mk_wdma(p, m):
        buf = wq_sl[p % 2][m]

        def f(e):
            src = wmats[m][:, 128 * p:128 * (p + 1)]
            return e.dma_start(
                buf.rearrange("p (c n) -> p c n", c=NDC),
                src.rearrange("(c p) n -> p c n", p=128))
        return f

    def rec_wdma(p):
        for m in range(3):
            sch.add("gpsimd", mk_wdma(p, m), writes=[r_wq[p % 2]],
                    dma=True, label=f"dma_w{p}{m}")

    rec_wdma(0)
    rec_wdma(1)

    rw8 = sb("rw8", [128, NDC * NE], BF16)
    r_rw8 = Res("rw8")
    sch.add("gpsimd",
            lambda e: e.dma_start(
                rw8[:, :].rearrange("p (c n) -> p c n", c=NDC),
                drw[:, :].rearrange("(c p) n -> p c n", p=128)),
            writes=[r_rw8], dma=True, label="dma_rw8")

    filler = []

    qp_sets = [(qpq, qpk, r_qpq, r_qpk), (qpqB, qpkB, r_qpq2, r_qpk2)]
    vp_sets = [(vp, r_vp), (vpB, r_vp2)]

    def queue_proj(p):
        wb = r_wq[p % 2]
        wsl = wq_sl[p % 2]
        qpq_, qpk_, r_qpq_, r_qpk_ = qp_sets[p % 2]
        vp_, r_vp_ = vp_sets[p % 2]
        items = []

        def qk_item(m, seg, dstap, rres, fb):
            def rec():
                def mmf(e):
                    last = None
                    for c in range(NDC):
                        last = e.matmul(
                            bank(fb, lo=0, hi=256),
                            wsl[m][:, 128 * c:128 * (c + 1)],
                            XT[:, 2048 * c + 256 * seg:
                               2048 * c + 256 * (seg + 1)],
                            start=(c == 0), stop=(c == NDC - 1))
                    return last
                sch.add("tensor", mmf, reads=[wb, r_XT], writes=[r_bank[fb]],
                        label=f"fproj{p}_{m}_{seg}")

                def evf(e):
                    return e.tensor_copy(
                        dstap[:, 256 * seg:256 * (seg + 1)],
                        bank(fb, lo=0, hi=256))
                sch.add("vector", evf, reads=[r_bank[fb]], writes=[rres],
                        label=f"fev{p}_{m}_{seg}")
            return rec

        def v_item(g, ti, fb):
            def rec():
                t = 4 * g + ti

                def mmf(e):
                    last = None
                    for c in range(NDC):
                        last = e.matmul(
                            bank(fb, lo=128 * ti, hi=128 * (ti + 1)),
                            XT[:, 2048 * c + 128 * t:
                               2048 * c + 128 * (t + 1)],
                            wsl[2][:, 128 * c:128 * (c + 1)],
                            start=(c == 0), stop=(c == NDC - 1))
                    return last

                def evf(e):
                    last = None
                    for m in range(2):
                        last = e.tensor_copy(
                            vp_[:, (m * NT + t) * 65:
                                (m * NT + t) * 65 + 64],
                            bank(fb, lo=128 * ti + 64 * m,
                                 hi=128 * ti + 64 * (m + 1)))
                    return last
                sch.add("tensor", mmf, reads=[wb, r_XT], writes=[r_bank[fb]],
                        label=f"fprojv{p}_{g}_{ti}")
                sch.add("vector", evf, reads=[r_bank[fb]], writes=[r_vp_],
                        label=f"fevv{p}_{g}_{ti}")
            return rec

        k = 0
        for m, dstap, rres in ((0, qpq_, r_qpq_), (1, qpk_, r_qpk_)):
            for seg in range(8):
                items.append(qk_item(m, seg, dstap, rres, 6 + (k % 2)))
                k += 1
        for g in range(4):
            for ti in range(4):
                items.append(v_item(g, ti, 6 + (k % 2)))
                k += 1
        return items

    def pop_filler(n=1):
        for _ in range(n):
            if filler:
                filler.pop(0)()

    pending_ep = []

    def flush_ep(n=-1):
        while pending_ep and n != 0:
            pending_ep.pop(0)()
            n -= 1

    def attn_head_quarter(p, m, qq, obuf):
        flush_ep(len(pending_ep) - 2)
        h = 2 * p + m
        qh0 = 512 * qq
        jmax = 4 * qq + 3
        hb = 64 * m
        ob = 4 + obuf
        qpq_, qpk_, r_qpq_, r_qpk_ = qp_sets[p % 2]
        vp_, r_vp_ = vp_sets[p % 2]

        def los(j):
            return max(qh0, 128 * j) - qh0

        def mk_S(j, sb_):
            lo = los(j)

            def f(e):
                return e.matmul(
                    bank(sb_, lo=lo, hi=512),
                    qpk_[hb:hb + 64, 128 * j:128 * (j + 1)],
                    qpq_[hb:hb + 64, qh0 + lo:qh0 + 512],
                    start=True, stop=True)
            return f

        def mk_exp(je, pb):
            lo = los(je)

            def f(e):
                return e.activation(Ptil[pb][:, lo:1024],
                                    PS[:, 1024 * pb + lo:1024 * pb + 1024],
                                    AF.Exp, bias=np.float32(-CSOFT),
                                    scale=np.float32(SMSCALE))
            return f

        def mk_PV(j, pb):
            lo = los(j)
            off = 512 * (j % 2)

            def f(e):
                return e.matmul(
                    bank(ob, p1=65, lo=lo, hi=512),
                    vp_[:, (m * NT + j) * 65:(m * NT + j + 1) * 65],
                    Ptil[pb][:, off + lo:off + 512],
                    start=(j == 0), stop=(j == jmax))
            return f

        npairs = (jmax + 1) // 2
        for pi in range(npairs):
            je, jo = 2 * pi, 2 * pi + 1
            pb = pi % 2
            sch.add("tensor", mk_S(je, 2 * pb), reads=[r_qpq_, r_qpk_, r_consts],
                    writes=[r_bank[2 * pb]], label=f"S{h}_{qq}_{je}")
            sch.add("tensor", mk_S(jo, 2 * pb + 1),
                    reads=[r_qpq_, r_qpk_, r_consts],
                    writes=[r_bank[2 * pb + 1]], label=f"S{h}_{qq}_{jo}")
            sch.add("scalar", mk_exp(je, pb),
                    reads=[r_bank[2 * pb], r_bank[2 * pb + 1]],
                    writes=[r_Ptil[pb]], label=f"exp{h}_{qq}_{pi}")
            if jo >= 4 * qq:
                def mk_msk(je=je, jo=jo, pb=pb):
                    def f(e):
                        last = None
                        for j in (je, jo):
                            if j >= 4 * qq:
                                lo = los(j)
                                off = 512 * (j % 2)
                                last = e.tensor_tensor(
                                    Ptil[pb][:, off + lo:off + lo + 128],
                                    Ptil[pb][:, off + lo:off + lo + 128],
                                    maskbin[:, :], OP.mult)
                        return last
                    return f
                sch.add("vector", mk_msk(), reads=[r_Ptil[pb], r_consts],
                        writes=[r_Ptil[pb]], label=f"msk{h}_{qq}_{pi}",
                        small=True)
            if pi >= 1:
                sch.add("tensor", mk_PV(2 * pi - 2, (pi - 1) % 2),
                        reads=[r_Ptil[(pi - 1) % 2], r_vp_],
                        writes=[r_bank[ob]], label=f"PV{h}_{qq}_{2*pi-2}")
                sch.add("tensor", mk_PV(2 * pi - 1, (pi - 1) % 2),
                        reads=[r_Ptil[(pi - 1) % 2], r_vp_],
                        writes=[r_bank[ob]], label=f"PV{h}_{qq}_{2*pi-1}")
            if pi >= 1:
                flush_ep(1)
            pop_filler(1)
        pb = (npairs - 1) % 2
        sch.add("tensor", mk_PV(jmax - 1, pb), reads=[r_Ptil[pb], r_vp_],
                writes=[r_bank[ob]], label=f"PV{h}_{qq}_{jmax-1}")
        sch.add("tensor", mk_PV(jmax, pb), reads=[r_Ptil[pb], r_vp_],
                writes=[r_bank[ob]], label=f"PV{h}_{qq}_{jmax}")
        def ep1():
            def f_sums(e):
                return e.tensor_copy(sumsb[0:1, 0:512], bank(ob, p0=64, p1=65))
            sch.add("vector", f_sums, reads=[r_bank[ob]], writes=[r_sums],
                    label=f"sums{h}_{qq}", small=True)

            def f_T(e):
                last = None
                for i in range(4):
                    last = e.matmul(bank(6, lo=i, hi=i + 1),
                                    sumsb[0:1, 128 * i:128 * (i + 1)],
                                    idn[0:1, 0:1], start=True, stop=True)
                return last
            sch.add("tensor", f_T, reads=[r_sums, r_consts],
                    writes=[r_bank[6]], label=f"epT{h}_{qq}")

            def f_recip(e):
                pass
        def ep2():
            def f_recip(e):
                with nc.allow_low_precision(reason="bf16 softmax recip"):
                    return e.reciprocal(recipTb[:, 0:4], bank(6, hi=4))
            sch.add("vector", f_recip, reads=[r_bank[6]], writes=[r_recip],
                    label=f"recip{h}_{qq}", small=True)

            def f_bc(e):
                last = None
                for i in range(4):
                    last = e.matmul(
                        bank(7, p1=64, lo=128 * i, hi=128 * (i + 1)),
                        recipTb[:, i:i + 1].broadcast_to([128, 64]),
                        idn[:, :], start=True, stop=True)
                return last
            sch.add("tensor", f_bc, reads=[r_recip, r_consts],
                    writes=[r_bank[7]], label=f"epbc{h}_{qq}")

            def f_evR(e):
                return e.tensor_copy(Rbv[0:64, 0:512], bank(7, p1=64))
            sch.add("vector", f_evR, reads=[r_bank[7]], writes=[r_Rb],
                    label=f"evR{h}_{qq}")

            ch = h // 2

            def f_mul(e):
                return e.tensor_tensor(
                    attnT[hb:hb + 64, 2048 * ch + qh0:2048 * ch + qh0 + 512],
                    bank(ob, p1=64), Rbv[0:64, 0:512], OP.mult)
            sch.add("vector", f_mul, reads=[r_bank[ob], r_Rb],
                    writes=[r_attnT], label=f"emul{h}_{qq}")
        pending_ep.append(ep1)
        pending_ep.append(ep2)

    obuf = 0
    for p in range(NH // 2):
        if p == 0:
            for it in queue_proj(0):
                it()
        if p + 1 < NH // 2:
            filler.extend(queue_proj(p + 1))
        if p + 2 < NH // 2:
            rec_wdma(p + 2)
        for m in range(2):
            for qq in range(4):
                attn_head_quarter(p, m, qq, obuf)
                obuf ^= 1
        while filler:
            pop_filler(1)
    flush_ep()

    for t in (14, 15):
        def f_xlate(t=t):
            def f(e):
                return e.dma_start(X32[:, DM * t:DM * (t + 1)],
                                   dx[128 * t:128 * (t + 1), :])
            return f
        sch.add("sync", f_xlate(), writes=[r_X32[t]], dma=True,
                label=f"dma_x32_{t}")

    pool = [(bankpair(0), [r_bank[0], r_bank[1]]),
            (bankpair(2), [r_bank[2], r_bank[3]]),
            (bankpair(4), [r_bank[4], r_bank[5]])]
    pooli = [0]

    def take_pool():
        ps, rr = pool[pooli[0] % 3]
        pooli[0] += 1
        return ps, rr

    LNEPS = 1e-5

    def ln_tasks(t, src_ps, rsrc_ps, sset, phase):
        hb = h1a if (t % 2 == 0) else h1b
        rhb = r_h1[t % 2]
        sl = slice(2 * 0 + sset, 2 * 0 + sset + 1)
        is2 = (phase == 2)
        accslot = t % 8

        def f_add(e):
            if is2:
                return e.scalar_tensor_tensor(
                    hb[:, :], ACC[:, DM * accslot:DM * (accslot + 1)],
                    wbuf[:, t:t + 1],
                    X32[:, DM * t:DM * (t + 1)], OP.mult, OP.add,
                    accum_out=st_s1[:, sset:sset + 1])
            return e.scalar_tensor_tensor(
                hb[:, :], src_ps[:, 0:1024], 1.0,
                X32[:, DM * t:DM * (t + 1)], OP.mult, OP.add,
                accum_out=st_s1[:, sset:sset + 1])
        reads = ([r_ACC[accslot]] if is2 else list(rsrc_ps)) + [r_X32[t]]
        if is2:
            reads.append(r_wbuf)
        sch.add("vector", f_add, reads=reads, writes=[rhb, r_stats[sset]],
                label=f"lnadd{phase}_{t}", small=True)

        def f_sq(e):
            return e.scalar_tensor_tensor(
                sqdump[:, :], hb[:, :], 1.0, hb[:, :], OP.mult, OP.mult,
                accum_out=st_s2[:, sset:sset + 1])
        sch.add("vector", f_sq, reads=[rhb],
                writes=[r_stats[sset], r_sqdump],
                label=f"lnsq{phase}_{t}", small=True)

        def f_nm(e):
            return e.activation(st_nm[:, sset:sset + 1],
                                st_s1[:, sset:sset + 1], AF.Copy,
                                scale=np.float32(-1.0 / DM))
        sch.add("scalar", f_nm, reads=[r_stats[sset]], writes=[r_stats[sset]],
                label=f"lnnm{phase}_{t}", small=True)

        def f_e2(e):
            return e.activation(st_var[:, sset:sset + 1],
                                st_s2[:, sset:sset + 1], AF.Copy,
                                scale=np.float32(1.0 / DM))
        sch.add("scalar", f_e2, reads=[r_stats[sset]], writes=[r_stats[sset]],
                label=f"lne2{phase}_{t}", small=True)

        def f_m2(e):
            return e.scalar_tensor_tensor(
                st_var[:, sset:sset + 1], st_nm[:, sset:sset + 1],
                st_nm[:, sset:sset + 1], st_var[:, sset:sset + 1],
                OP.mult, OP.subtract)
        sch.add("vector", f_m2, reads=[r_stats[sset]],
                writes=[r_stats[sset]], label=f"lnm2{phase}_{t}", small=True)

        def f_ln(e):
            return e.activation(st_ln[:, sset:sset + 1],
                                st_var[:, sset:sset + 1], AF.Ln,
                                bias=np.float32(LNEPS),
                                scale=np.float32(-1.0))
        sch.add("scalar", f_ln, reads=[r_stats[sset]], writes=[r_stats[sset]],
                label=f"lnln{phase}_{t}", small=True)

        def f_rstd(e):
            return e.activation(st_rstd[:, sset:sset + 1],
                                st_ln[:, sset:sset + 1], AF.Exp,
                                scale=np.float32(-0.5))
        sch.add("scalar", f_rstd, reads=[r_stats[sset]], writes=[r_stats[sset]],
                label=f"lnrstd{phase}_{t}", small=True)

        def f_nmr(e):
            return e.tensor_scalar(st_nmr[:, sset:sset + 1],
                                   st_nm[:, sset:sset + 1],
                                   st_rstd[:, sset:sset + 1], None, OP.mult)
        sch.add("vector", f_nmr, reads=[r_stats[sset]], writes=[r_stats[sset]],
                label=f"lnnmr{phase}_{t}", small=True)

        if not is2:
            def f_norm(e):
                return e.activation(X32[:, DM * t:DM * (t + 1)], hb[:, :],
                                    AF.Identity, bias=st_nmr[:, sset:sset + 1],
                                    scale=st_rstd[:, sset:sset + 1])
            sch.add("scalar", f_norm, reads=[rhb, r_stats[sset]],
                    writes=[r_X32[t]], label=f"lnnorm{phase}_{t}")
        else:
            ob = t % 2

            def f_norm(e):
                return e.activation(outf[ob][:, :], hb[:, :],
                                    AF.Identity, bias=st_nmr[:, sset:sset + 1],
                                    scale=st_rstd[:, sset:sset + 1])
            sch.add("scalar", f_norm, reads=[rhb, r_stats[sset]],
                    writes=[r_out[ob]], label=f"lnnorm{phase}_{t}")

            def f_st(e):
                return e.dma_start(dy[128 * t:128 * (t + 1), :],
                                   outf[ob][:, :])
            sch.add("sync", f_st, reads=[r_out[ob]], dma=True,
                    label=f"store{t}")

    def rec_oln(t):
        ps, rps = take_pool()

        def f_oproj(t=t, ps=ps):
            def f(e):
                last = None
                for c in range(NDC):
                    for sg in range(2):
                        last = e.matmul(
                            ps[:, 512 * sg:512 * (sg + 1)],
                            attnT[:, 2048 * c + 128 * t:2048 * c + 128 * (t + 1)],
                            wosb[:, 1024 * c + 512 * sg:1024 * c + 512 * (sg + 1)],
                            start=(c == 0), stop=(c == NDC - 1))
                return last
            return f
        sch.add("tensor", f_oproj(), reads=[r_attnT, r_wo], writes=list(rps),
                label=f"oproj{t}")
        ln_tasks(t, ps, rps, t % 2, phase=1)

    def rec_dphase(t):
        ps, rps = take_pool()

        def f_xt(t=t, ps=ps):
            def f(e):
                last = None
                for c in range(NDC):
                    last = e.matmul(
                        ps[:, 128 * c:128 * (c + 1)],
                        X32[:, DM * t + 128 * c:DM * t + 128 * (c + 1)],
                        idnf[:, :], start=True, stop=True)
                return last
            return f
        sch.add("tensor", f_xt(), reads=[r_X32[t], r_consts], writes=list(rps),
                label=f"xmt{t}")

        def f_evxt(t=t, ps=ps):
            def f(e):
                return e.activation(
                    XT[:, :].rearrange("p (c n) -> p c n", c=NDC)[:, :, 128 * t:128 * (t + 1)],
                    ps[:, 0:1024].rearrange("p (c n) -> p c n", c=NDC),
                    AF.Copy)
            return f
        sch.add("scalar", f_evxt(), reads=list(rps), writes=[r_XTs[t]],
                label=f"evxmt{t}")

        lb = logitA if t % 2 == 0 else logitB
        rlb = r_logit[t % 2]

        rbk = 6 + (t % 2)

        def f_rmm(t=t, rbk=rbk):
            def f(e):
                last = None
                for c in range(NDC):
                    last = e.matmul(
                        bank(rbk, hi=NE),
                        XT[:, 2048 * c + 128 * t:2048 * c + 128 * (t + 1)],
                        rw8[:, NE * c:NE * (c + 1)],
                        start=(c == 0), stop=(c == NDC - 1))
                return last
            return f
        sch.add("tensor", f_rmm(), reads=[r_XTs[t], r_rw8],
                writes=[r_bank[rbk]], label=f"rmm{t}")

        def f_rev(lb=lb, rbk=rbk):
            def f(e):
                return e.tensor_add(lb[:, :], bank(rbk, hi=NE), rbrep[:, :])
            return f
        sch.add("vector", f_rev(), reads=[r_bank[rbk], r_consts], writes=[rlb],
                label=f"rev{t}", small=True)

        s_ = t % 2
        sr = 2 + s_

        def f_rtree(lb=lb, s_=s_, sr=sr):
            def f(e):
                e.tensor_max(mscr[:, 6 * s_:6 * s_ + 4], lb[:, 0:4], lb[:, 4:8])
                e.drain(fusable=True)
                e.tensor_max(mscr[:, 6 * s_ + 4:6 * s_ + 6],
                             mscr[:, 6 * s_:6 * s_ + 2],
                             mscr[:, 6 * s_ + 2:6 * s_ + 4])
                e.drain(fusable=True)
                return e.tensor_max(st_s2[:, sr:sr + 1],
                                    mscr[:, 6 * s_ + 4:6 * s_ + 5],
                                    mscr[:, 6 * s_ + 5:6 * s_ + 6])
            return f
        sch.add("vector", f_rtree(), reads=[rlb], writes=[r_stats[sr]],
                label=f"rtree{t}", small=True)

        def f_rsub(lb=lb, s_=s_, sr=sr):
            def f(e):
                return e.tensor_scalar(lbm[:, 8 * s_:8 * s_ + 8], lb[:, :],
                                       st_s2[:, sr:sr + 1], None, OP.subtract)
            return f
        sch.add("vector", f_rsub(), reads=[rlb, r_stats[sr]],
                writes=[r_stats[sr]], label=f"rsub{t}", small=True)

        def f_rexp(s_=s_):
            def f(e):
                return e.activation(exd[:, 8 * s_:8 * s_ + 8],
                                    lbm[:, 8 * s_:8 * s_ + 8], AF.Exp)
            return f
        sch.add("scalar", f_rexp(), reads=[r_stats[sr]], writes=[r_stats[sr]],
                label=f"rexp{t}", small=True)

        def f_rsum(s_=s_, sr=sr):
            def f(e):
                return e.scalar_tensor_tensor(
                    dump8[:, :], exd[:, 8 * s_:8 * s_ + 8], 1.0,
                    exd[:, 8 * s_:8 * s_ + 8], OP.mult, OP.bypass,
                    accum_out=st_var[:, sr:sr + 1])
            return f
        sch.add("vector", f_rsum(), reads=[r_stats[sr]], writes=[r_stats[sr]],
                label=f"rsum{t}", small=True)

        def f_rmask(s_=s_, t=t):
            def f(e):
                return e.tensor_single_scalar(maskw[:, NE * t:NE * (t + 1)],
                                              lbm[:, 8 * s_:8 * s_ + 8],
                                              0.0, OP.is_ge)
            return f
        sch.add("vector", f_rmask(), reads=[r_stats[sr]],
                writes=[r_maskw[t]], label=f"rmask{t}", small=True)

        def f_wrecip(sr=sr, t=t):
            def f(e):
                return e.reciprocal(wbuf[:, t:t + 1], st_var[:, sr:sr + 1])
            return f
        sch.add("vector", f_wrecip(), reads=[r_stats[sr]], writes=[r_wbuf],
                label=f"wrecip{t}", small=True)

    def rec_ewdma(li, ei):
        b = li % 2

        def f(e):
            return e.dma_start(
                ewbuf(b).rearrange("p (c n) -> p c n", c=NDC),
                dew[ei].rearrange("(c p) n -> p c n", p=128))
        sch.add("gpsimd", f, writes=[r_ew[b]], dma=True,
                label=f"dma_ew{li}")

    def rec_mcast(t):
        rmt = r_mwt[t % 2]

        def f_mcast(e):
            return e.activation(maskwbf[:, :], maskw[:, NE * t:NE * (t + 1)],
                                AF.Copy)
        sch.add("scalar", f_mcast, reads=[r_maskw[t]], writes=[rmt],
                label=f"mcast{t}", small=True)

    def rec_tile_init(t, li):
        rec_mcast(t)
        mt = mwtA if t % 2 == 0 else mwtB
        rmt = r_mwt[t % 2]
        accslot = t % 8
        mbk = 6 + (t % 2)
        eb_ = li % 2

        ps_e, rps_e = take_pool()

        def f_emm(t=t, ps=ps_e, eb_=eb_):
            def f(e):
                last = None
                buf = ewbuf(eb_)
                for c in range(NDC):
                    for sg in range(2):
                        last = e.matmul(
                            ps[:, 512 * sg:512 * (sg + 1)],
                            XT[:, 2048 * c + 128 * t:
                               2048 * c + 128 * (t + 1)],
                            buf[:, 1024 * c + 512 * sg:
                                1024 * c + 512 * (sg + 1)],
                            start=(c == 0), stop=(c == NDC - 1))
                return last
            return f
        sch.add("tensor", f_emm(), reads=[r_XTs[t], r_ew[eb_]],
                writes=list(rps_e), label=f"emm{li}_{t}")

        def f_mT(e, mbk=mbk):
            return e.matmul(bank(mbk, p1=NE, lo=256, hi=384), maskwbf[:, :],
                            idn[:, :], start=True, stop=True)
        sch.add("tensor", f_mT, reads=[rmt, r_consts], writes=[r_bank[mbk]],
                label=f"mT{t}")

        def f_mev(e, mbk=mbk):
            return e.activation(mt[:, :], bank(mbk, p1=NE, lo=256, hi=384),
                                AF.Copy)
        sch.add("scalar", f_mev, reads=[r_bank[mbk]], writes=[rmt],
                label=f"mev{t}", small=True)

        ps_b, rps_b = take_pool()

        def f_bias(ps=ps_b):
            def f(e):
                last = None
                for sg in range(2):
                    last = e.matmul(ps[:, 512 * sg:512 * (sg + 1)],
                                    mt[:, :],
                                    ebsb[:, 512 * sg:512 * (sg + 1)],
                                    start=True, stop=True)
                return last
            return f
        sch.add("tensor", f_bias(), reads=[rmt, r_ebsb], writes=list(rps_b),
                label=f"bias{t}")

        bstg = arA[:, 29696:30720] if t % 2 == 0 else arA[:, 31232:32256]
        rbs = r_bstg[t % 2]

        def f_bst(e, ps=ps_b, bstg=bstg):
            return e.activation(bstg, ps[:, 0:1024], AF.Copy)
        sch.add("scalar", f_bst, reads=list(rps_b), writes=[rbs],
                label=f"bstg{t}")

        def f_estt(t=t, pe=ps_e, bstg=bstg, accslot=accslot):
            def f(e):
                return e.scalar_tensor_tensor(
                    ACC[:, DM * accslot:DM * (accslot + 1)],
                    pe[:, 0:1024],
                    maskw[:, NE * t:NE * t + 1],
                    bstg,
                    OP.mult, OP.add)
            return f
        sch.add("vector", f_estt(),
                reads=list(rps_e) + [rbs, r_maskw[t]],
                writes=[r_ACC[accslot]], label=f"estt{li}_{t}")

    def rec_expert_tile(li, e_i, t, init=False):
        eb_ = li % 2
        ps, rps = take_pool()
        accslot = t % 8

        def f_emm(t=t, ps=ps, eb_=eb_):
            def f(e):
                last = None
                buf = ewbuf(eb_)
                for c in range(NDC):
                    for sg in range(2):
                        last = e.matmul(
                            ps[:, 512 * sg:512 * (sg + 1)],
                            XT[:, 2048 * c + 128 * t:
                               2048 * c + 128 * (t + 1)],
                            buf[:, 1024 * c + 512 * sg:
                                1024 * c + 512 * (sg + 1)],
                            start=(c == 0), stop=(c == NDC - 1))
                return last
            return f
        sch.add("tensor", f_emm(),
                reads=[r_XTs[t], r_ew[eb_]],
                writes=list(rps), label=f"emm{li}_{t}")

        def f_estt(t=t, ps=ps, e_i=e_i, accslot=accslot, init=init):
            def f(e):
                return e.scalar_tensor_tensor(
                    ACC[:, DM * accslot:DM * (accslot + 1)],
                    ps[:, 0:1024],
                    maskw[:, NE * t + e_i:NE * t + e_i + 1],
                    ACC[:, DM * accslot:DM * (accslot + 1)],
                    OP.mult, OP.bypass if init else OP.add)
            return f
        rdd = list(rps) + [r_maskw[t]]
        if not init:
            rdd.append(r_ACC[accslot])
        sch.add("vector", f_estt(), reads=rdd,
                writes=[r_ACC[accslot]], label=f"estt{li}_{t}")

    for t in range(NT):
        rec_oln(t)
    sch.add("gpsimd", lambda e: e.dma_start(ebsb, deb[:, :]),
            writes=[r_ebsb], dma=True, label="dma_eb")
    rec_ewdma(0, 0)
    for t in range(8):
        rec_dphase(t)
        rec_tile_init(t, 0)
    rec_ewdma(1, 1)
    for e_i in range(1, NE):
        li = e_i
        rec_dphase(7 + e_i)
        for t in range(8):
            rec_expert_tile(li, e_i, t)
            if e_i == NE - 1 and t == 3:
                rec_dphase(15)
        rec_ewdma(li + 1, (li + 1) % NE)
    for t in range(8, 16):
        ln_tasks(t - 8, None, None, (t - 8) % 2, phase=2)
        rec_tile_init(t, 8)
    rec_ewdma(9, 1)
    for e_i in range(1, NE):
        li = 8 + e_i
        for t in range(8, 16):
            rec_expert_tile(li, e_i, t)
            if e_i == NE - 1:
                ln_tasks(t, None, None, t % 2, phase=2)
        if li + 1 < 16:
            rec_ewdma(li + 1, (li + 1) % NE)

    store_tasks = [t for t in sch.per_engine["sync"] if t.label.startswith("store")]

    def f_fin(e):
        return None
    fin = Task("sync", f_fin, False, "final")
    fin.rec_idx = len(sch.tasks)
    fin.deps = store_tasks
    sch.tasks.append(fin)
    sch.per_engine["sync"].append(fin)

    with ctx:
        sem_t = ctx.enter_context(nc.semaphore("sem_t"))
        sem_v = ctx.enter_context(nc.semaphore("sem_v"))
        sem_s = ctx.enter_context(nc.semaphore("sem_s"))
        sem_g = ctx.enter_context(nc.semaphore("sem_g"))
        sem_sp = ctx.enter_context(nc.semaphore("sem_sp"))
        dma_sems = [ctx.enter_context(nc.semaphore(f"dq{i}"))
                    for i in range(len(sch.dma_tasks))]
        with nc.Block() as block:
            sems = {"tensor": sem_t, "vector": sem_v, "scalar": sem_s,
                    "sync": sem_sp, "gpsimd": sem_g}
            sch.finalize(nc, block, sems, dma_sems)
    return nc


_NC_CACHE = None


def kernel(L0_attn_x, L0_attn_W_Q, L0_attn_W_K, L0_attn_W_V, L0_attn_W_O,
           L0_ffn_router_w, L0_ffn_router_b, L0_ffn_expert_w, L0_ffn_expert_b):
    global _NC_CACHE
    x = np.ascontiguousarray(np.asarray(L0_attn_x, dtype=np.float32))
    wq_t = np.ascontiguousarray(np.asarray(L0_attn_W_Q, np.float32).T)
    wk_t = np.ascontiguousarray(np.asarray(L0_attn_W_K, np.float32).T)
    wv_t = np.ascontiguousarray(np.asarray(L0_attn_W_V, np.float32).T)
    wo_t = np.ascontiguousarray(np.asarray(L0_attn_W_O, np.float32).T)
    rw_t = np.ascontiguousarray(np.asarray(L0_ffn_router_w, np.float32).T)
    rb_rep = np.ascontiguousarray(
        np.broadcast_to(np.asarray(L0_ffn_router_b, np.float32), (128, NE)))
    ew_t = np.ascontiguousarray(
        np.asarray(L0_ffn_expert_w, np.float32).transpose(0, 2, 1))
    eb = np.ascontiguousarray(np.asarray(L0_ffn_expert_b, np.float32))

    idn_bf = np.eye(128, dtype=BFNP)
    idn_f32 = np.eye(128, dtype=np.float32)
    kk, qq = np.meshgrid(np.arange(128), np.arange(128), indexing="ij")
    maskneg = np.where(qq >= kk, 0.0, -1e9).astype(BFNP)
    maskbin = np.where(qq >= kk, 1.0, 0.0).astype(BFNP)

    if _NC_CACHE is None:
        _NC_CACHE = build_program()
    nc = _NC_CACHE

    in_maps = []
    for b in range(8):
        xb = np.ascontiguousarray(x[b])
        in_maps.append({
            "x": xb,
            "x_t": np.ascontiguousarray(xb.T),
            "wq_t": wq_t, "wk_t": wk_t, "wv_t": wv_t, "wo_t": wo_t,
            "rw_t": rw_t, "rb_rep": rb_rep, "ew_t": ew_t, "eb": eb,
            "idn_bf": idn_bf, "idn_f32": idn_f32, "maskneg": maskneg,
            "maskbin": maskbin,
        })
    res = run_bass_kernel_spmd(nc, in_maps, core_ids=list(range(8)))
    out = np.stack([np.asarray(res.results[b]["y"], dtype=np.float32)
                    for b in range(8)], axis=0)
    return out
